# version 38
# baseline (speedup 1.0000x reference)
import contextlib
import numpy as np

import concourse.bass as bass
import concourse.mybir as mybir
from concourse.bass_utils import run_bass_kernel_spmd

F32 = mybir.dt.float32
F16 = mybir.dt.float16
F8 = mybir.dt.float8e4
AF = mybir.ActivationFunctionType
ALU = mybir.AluOpType

D = 1024
H = 16
FD = 16
HD = 64
NFT = D // 128
PHI = 152
EPS = 1e-5
LN_EPS = 1e-5
SUB = 128
MMN = 256

PAIR_OFF = [0]
for _a in range(16):
    PAIR_OFF.append(PAIR_OFF[-1] + (16 - _a))


def _ceil(a, b):
    return (a + b - 1) // b


class Prog:

    ENGS = ("t", "v", "s", "q", "g")

    def __init__(self, nc):
        self.nc = nc
        self.tasks = {e: [] for e in self.ENGS}
        self.sems = {}
        self.counts = {e: 0 for e in self.ENGS}
        self.dma_counts = {"qd": 0, "gd": 0, "gd2": 0, "st": 0, "st1": 0,
                           "rpq": 0, "rpk": 0}

    def make_sems(self, ent):
        for e in self.ENGS:
            self.sems[e] = ent(self.nc.semaphore(f"sem_{e}"))
            self.sems["ss_" + e] = ent(self.nc.semaphore(f"ssem_{e}"))
        for d in ("qd", "gd", "gd2", "st", "st1", "rpq", "rpk"):
            self.sems[d] = ent(self.nc.semaphore(f"sem_{d}"))
        self.self_cnt = {e: 0 for e in self.ENGS}

    def chain(self, e, eng, instr):
        self.self_cnt[eng] += 1
        instr.then_inc(self.sems["ss_" + eng], 1)
        e.wait_ge(self.sems["ss_" + eng], self.self_cnt[eng])

    def add(self, eng, deps, fn):
        self.tasks[eng].append((list(deps), fn, None))
        self.counts[eng] += 1
        return (self.sems[eng], self.counts[eng])

    def dma(self, eng, deps, fn, sem_name, n=1):
        self.dma_counts[sem_name] += n
        self.tasks[eng].append((list(deps), fn, (sem_name, n)))
        return (self.sems[sem_name], 16 * self.dma_counts[sem_name])

    def emit(self, block):
        nc = self.nc

        class _Rec:
            def __init__(self, eng):
                self._e = eng
                self.last = None

            def __getattr__(self, n):
                a = getattr(self._e, n)
                if callable(a):
                    def w(*args, **kw):
                        r = a(*args, **kw)
                        if hasattr(r, "then_inc"):
                            self.last = r
                        return r
                    return w
                return a

        def mk(e):
            def body(eng):
                waited = {}
                rec = _Rec(eng)
                n_done = 0
                for deps, fn, dmainfo in self.tasks[e]:
                    if n_done > 0 and e in ("v", "s"):
                        eng.wait_ge(self.sems[e], n_done)
                    n_done += 1
                    for (sem, val) in deps:
                        key = id(sem)
                        if waited.get(key, -1) < val:
                            eng.wait_ge(sem, val)
                            waited[key] = val
                    rec.last = None
                    ret = fn(rec)
                    if dmainfo is None:
                        last = ret if ret is not None else rec.last
                        if last is None:
                            last = eng.nop()
                        last.then_inc(self.sems[e], 1)
            return body

        for e, attr in (("t", "tensor"), ("v", "vector"), ("s", "scalar"),
                        ("q", "sync"), ("g", "gpsimd")):
            if self.tasks[e]:
                getattr(block, attr)(mk(e))


def build(L=4096, SC=256, acc_group=16):
    NSC = L // SC
    NSUB = SC // SUB
    NNH = _ceil(SC, MMN)
    nc = bass.Bass("TRN2", target_bir_lowering=False, debug=False)

    xT = nc.dram_tensor("xT", [D, L], F32, kind="ExternalInput").ap()
    wq = nc.dram_tensor("wq", [D, 256], F32, kind="ExternalInput").ap()
    wk = nc.dram_tensor("wk", [D, 256], F32, kind="ExternalInput").ap()
    wv = nc.dram_tensor("wv", [D, D], F32, kind="ExternalInput").ap()
    wo = nc.dram_tensor("wo", [D, D], F32, kind="ExternalInput").ap()
    wf = nc.dram_tensor("wf", [D, D], F32, kind="ExternalInput").ap()
    g_col = nc.dram_tensor("g_col", [128, 1], F32, kind="ExternalInput").ap()
    b_col = nc.dram_tensor("b_col", [128, 1], F32, kind="ExternalInput").ap()
    bias_c = nc.dram_tensor("bias_c", [128, 8], F32, kind="ExternalInput").ap()
    s_io = nc.dram_tensor("s_io", [1, 2], F32, kind="ExternalInput").ap()
    tri_h = nc.dram_tensor("tri", [128, 384], F16, kind="ExternalInput").ap()
    phco_h = nc.dram_tensor("phco", [128, 2], F32, kind="ExternalInput").ap()
    pil_h = nc.dram_tensor("pil", [128, 128], F16, kind="ExternalInput").ap()
    id16_h = nc.dram_tensor("id16", [128, 128], F16, kind="ExternalInput").ap()
    ind8_h = nc.dram_tensor("ind8", [8, 128], F16, kind="ExternalInput").ap()
    onesg_h = nc.dram_tensor("onesg", [128, 8], F16, kind="ExternalInput").ap()
    e0_h = nc.dram_tensor("e0", [128, 128], F16, kind="ExternalInput").ap()
    o16c_h = nc.dram_tensor("o16c", [128, 1], F16, kind="ExternalInput").ap()
    cst_h = nc.dram_tensor("csth", [128, 9], F32, kind="ExternalInput").ap()
    yT = nc.dram_tensor("yT", [D, L], F32, kind="ExternalOutput").ap()

    with contextlib.ExitStack() as stack:
        ent = stack.enter_context
        sb = lambda name, shape, dt=F32: ent(nc.sbuf_tensor(name, shape, dt))
        psmaker = lambda name, shape, dt=F32: ent(nc.psum_tensor(name, shape, dt))

        xs2 = [sb(f"xs{i}", [128, NFT * SC]) for i in range(2)]
        x162 = [sb(f"x16_{i}", [128, NFT * SC], F16) for i in range(2)]
        wq16 = sb("wq16", [128, 8 * 256], F16)
        wk16 = sb("wk16", [128, 8 * 256], F16)
        wv16 = sb("wv16", [128, 8 * 1024], F16)
        wo16 = sb("wo16", [128, 8 * 1024], F16)
        wf16 = sb("wf16", [128, 8 * 1024], F16)
        wq1x = sb("wq1x", [128, 256], F16)
        wk1x = sb("wk1x", [128, 256], F16)
        wv1x = sb("wv1x", [128, 1024], F16)
        negmux2 = [sb(f"negmux_{i}", [128, SC], F16) for i in range(2)]
        epscx2 = [sb(f"epscx_{i}", [128, SC], F16) for i in range(2)]
        nm2x = sb("nm2x", [128, SC], F16)
        rs2x = sb("rs2x", [128, SC], F16)
        srow2 = [sb(f"srow_{i}", [1, 3 * SC]) for i in range(2)]
        rc2 = [sb(f"rc_{i}", [128, 4]) for i in range(2)]
        fmu2 = [sb(f"fmu_{i}", [8, 4 * SC], F16) for i in range(2)]
        fvarS = sb("fvar_s", [8, 4 * SC], F16)
        flnxS = sb("flnx_s", [8, 4 * SC])
        fR2 = [sb(f"fR_{i}", [8, 4 * SC], F16) for i in range(2)]
        q162 = [sb(f"q16_{i}", [128, 2 * SC], F16) for i in range(2)]
        k162 = [sb(f"k16_{i}", [128, 2 * SC], F16) for i in range(2)]
        qh2 = [sb(f"qh_{i}", [128, 2 * SC], F16) for i in range(2)]
        kh2 = [sb(f"kh_{i}", [128, 2 * SC], F16) for i in range(2)]
        qhp2 = [sb(f"qhp{i}", [128, 6 * SC], F16) for i in range(2)]
        khp2 = [sb(f"khp{i}", [128, 6 * SC], F16) for i in range(2)]
        qil = sb("qil", [128, NSUB * 256], F16)
        kil = sb("kil", [128, NSUB * 256], F16)
        phq = sb("phq", [128, NSUB * PHI * 16], F16)
        phk = sb("phk", [128, NSUB * PHI * 16], F16)
        phqT = sb("phqT", [128, 2 * 512], F16)
        vt2 = [sb(f"vt_{i}", [128, NSUB * H * 65], F16) for i in range(2)]
        SSW = sum(SC - kb * 128 for kb in range(SC // SUB))
        ssq = sb("ssq", [128, 2 * SSW], F16)
        Sa = sb("Sa", [128, H * 65])
        Sb = sb("Sb", [24, H * 65])
        snapa = [sb(f"snapa{i}", [128, H * 65], F16) for i in range(2)]
        snapb = [sb(f"snapb{i}", [24, H * 65], F16) for i in range(2)]
        o162S = sb("o16_s", [128, NSUB * 1024], F16)
        o162 = [o162S, o162S]
        xq16 = sb("xq16", [128, NFT * SC], F16)
        oT = sb("oT", [128, 8 * SC], F16)
        t1f = sb("t1f", [128, 2 * SC], F16)
        t1q = sb("t1q", [128, 4 * SC], F16)

        q8f = sb("q8f", [128, 4 * SC], F8)
        ybuf = sb("ybuf", [128, 2 * SC])
        xsq2 = [sb(f"xsq_{i}", [128, NFT * SC], F16) for i in range(2)]
        sq32 = sb("sq32", [128, 2 * SSW])
        tri = sb("tri_s", [128, 384], F16)
        phcoc = sb("phcoc", [128, 2], F32)
        pil = sb("pil_s", [128, 128], F16)
        id16 = sb("id16_s", [128, 128], F16)
        ind8 = sb("ind8_s", [8, 128], F16)
        onesg = sb("onesg_s", [128, 8], F16)
        e0 = sb("e0_s", [128, 128], F16)
        o16c = sb("o16c_s", [128, 1], F16)
        cstc = sb("cstc", [128, 9])
        gq = sb("gq", [128, 1])
        bq = sb("bq", [128, 1])
        gk = sb("gk", [128, 1])
        bk = sb("bk", [128, 1])
        biasc = sb("biasc", [128, 8])
        sioc = sb("sioc", [1, 2])
        sinv = sb("sinv", [128, 1])
        ssc = sb("ssc", [128, 1])
        zsc = sb("zsc", [128, 8])

        (C_LNEPS, C_QSC, C_ONE, C_Z, C_HALFNEG, C_NEG1, C_CLIP, C_EPS16,
         C_EPSSQ) = range(9)
        nc.const_aps.register(F32, LN_EPS, cstc[:, C_LNEPS:C_LNEPS + 1])
        nc.const_aps.register(F32, 0.25, cstc[:, C_QSC:C_QSC + 1])
        nc.const_aps.register(F32, 1.0, cstc[:, C_ONE:C_ONE + 1])
        nc.const_aps.register(F32, 0.0, cstc[:, C_Z:C_Z + 1])
        nc.const_aps.register(F32, -0.5, cstc[:, C_HALFNEG:C_HALFNEG + 1])
        nc.const_aps.register(F32, -1.0, cstc[:, C_NEG1:C_NEG1 + 1])
        nc.const_aps.register(F32, 240.0, cstc[:, C_CLIP:C_CLIP + 1])
        nc.const_aps.register(F32, 16.0 * LN_EPS, cstc[:, C_EPS16:C_EPS16 + 1])
        nc.const_aps.register(F32, 16.0 * LN_EPS * LN_EPS,
                              cstc[:, C_EPSSQ:C_EPSSQ + 1])

        ps = [psmaker(f"ps{i}", [128, 512]) for i in range(5)]
        pstT = psmaker("pstT", [128, 1024], F16)
        pstT2 = psmaker("pstT2", [128, 1024], F16)
        ps7 = psmaker("ps7", [128, 512])

        P = Prog(nc)
        P.make_sems(ent)

        def t_init_consts(e):
            e.dma_start(out=tri[:, :], in_=tri_h).then_inc(P.sems["qd"], 16)
            e.dma_start(out=id16[:, :], in_=id16_h).then_inc(P.sems["qd"], 16)
            e.dma_start(out=ind8[:, :], in_=ind8_h).then_inc(P.sems["qd"], 16)
            e.dma_start(out=onesg[:, :], in_=onesg_h).then_inc(P.sems["qd"], 16)
            e.dma_start(out=e0[:, :], in_=e0_h).then_inc(P.sems["qd"], 16)
            e.dma_start(out=o16c[:, :], in_=o16c_h).then_inc(P.sems["qd"], 16)
            e.dma_start(out=cstc[:, :], in_=cst_h).then_inc(P.sems["qd"], 16)
            e.dma_start(out=gq[:, :], in_=g_col).then_inc(P.sems["qd"], 16)
            e.dma_start(out=gk[:, :], in_=g_col).then_inc(P.sems["qd"], 16)
            e.dma_start(out=bq[:, :], in_=b_col).then_inc(P.sems["qd"], 16)
            e.dma_start(out=bk[:, :], in_=b_col).then_inc(P.sems["qd"], 16)
            e.dma_start(out=biasc[:, :], in_=bias_c).then_inc(P.sems["qd"], 16)
            e.dma_start(out=sioc[:, :], in_=s_io).then_inc(P.sems["qd"], 16)
            e.dma_start(out=phcoc[:, :], in_=phco_h).then_inc(P.sems["qd"], 16)
            e.dma_start(out=pil[:, :], in_=pil_h).then_inc(P.sems["qd"], 16)
        tk_consts = P.dma("q", [], t_init_consts, "qd", 15)

        def t_init_wqkv(e):
            for dst, srct, nf in ((wq16, wq, 256), (wk16, wk, 256),
                                  (wv16, wv, 1024)):
                e.dma_start(
                    out=dst[:, :].rearrange("p (kt f) -> p kt f", f=nf),
                    in_=srct.rearrange("(kt p) f -> p kt f", p=128),
                ).then_inc(P.sems["gd"], 16)
        tk_w = P.dma("g", [], t_init_wqkv, "gd", 3)

        def t_init_wof(e):
            for dst, srct, nf in ((wo16, wo, 1024), (wf16, wf, 1024)):
                e.dma_start(
                    out=dst[:, :].rearrange("p (kt f) -> p kt f", f=nf),
                    in_=srct.rearrange("(kt p) f -> p kt f", p=128),
                ).then_inc(P.sems["gd2"], 16)
        tk_wof = P.dma("g", [], t_init_wof, "gd2", 2)

        def t_init_zero(e):
            for i in range(2):
                e.memset(negmux2[i][:, :], 0.0)
                e.memset(epscx2[i][:, :], 0.0)
            e.memset(nm2x[:, :], 0.0)
            e.memset(rs2x[:, :], 0.0)
            e.memset(wq1x[:, :], 0.0)
            e.memset(wk1x[:, :], 0.0)
            e.memset(wv1x[:, :], 0.0)
            e.memset(Sa[:, :], 0.0)
            e.memset(Sb[:, :], 0.0)
            for i in range(2):
                e.memset(qhp2[i][:, :], 0.0)
                e.memset(khp2[i][:, :], 0.0)
            for i in range(2):
                ap = vt2[i][:, :].rearrange("p (n h c) -> p n h c",
                                            n=NSUB, h=H)[:, :, :, 64:65]
                e.memset(ap, 1.0)
            return None
        tk_zero = P.add("v", [], t_init_zero)

        def t_init_scale(e):
            e.activation(gq[:, :], gq[:, :], AF.Identity, scale=0.25)
            e.activation(bq[:, :], bq[:, :], AF.Identity, scale=0.25)
            return None
        tk_scale = P.add("s", [tk_consts], t_init_scale)

        def t_init_sio_rows(e):
            P.chain(e, "s", e.activation(srow2[0][:, 0:1], sioc[:, 0:1], AF.Ln))
            e.activation(srow2[0][:, 0:1], srow2[0][:, 0:1], AF.Exp,
                         scale=cstc[0:1, C_NEG1:C_NEG1 + 1])
            return None
        tk_sio1 = P.add("s", [tk_consts, tk_scale], t_init_sio_rows)

        def t_init_sio_mul(e):
            P.chain(e, "v", e.tensor_tensor(srow2[0][:, 1:2], sioc[:, 0:1],
                                            sioc[:, 1:2], ALU.mult))
            return e.tensor_copy(negmux2[0][0:1, 0:2], srow2[0][0:1, 0:2])
        tk_sio2 = P.add("v", [tk_sio1, tk_zero], t_init_sio_mul)

        def t_init_sio_bcast(e):
            e.matmul(ps7[:, 0:2], e0[:, 0:128], negmux2[0][:, 0:2],
                     start=True, stop=True)
            return None
        tk_sio3 = P.add("t", [tk_sio2, tk_consts], t_init_sio_bcast)

        def t_init_sio_evac(e):
            e.tensor_copy(sinv[:, :], ps7[:, 0:1])
            e.tensor_copy(ssc[:, :], ps7[:, 1:2])
            return e.memset(negmux2[0][0:1, 0:2], 0.0)
        tk_sio4 = P.add("v", [tk_sio3], t_init_sio_evac)

        def t_init_w1qk(e):
            for w16t, lo in ((wq16, 0), (wk16, 256)):
                for kt in range(8):
                    e.matmul(ps7[0:1, lo:lo + 256], o16c[:, 0:1],
                             w16t[:, kt * 256:kt * 256 + 256],
                             start=(kt == 0), stop=(kt == 7))
            return None
        tk_w1qk = P.add("t", [tk_w, tk_consts, tk_sio4], t_init_w1qk)

        def t_init_w1qk_evac(e):
            e.tensor_copy(wq1x[0:1, :], ps7[0:1, 0:256])
            return e.tensor_copy(wk1x[0:1, :], ps7[0:1, 256:512])
        tk_w1qke = P.add("v", [tk_w1qk, tk_zero, tk_sio4], t_init_w1qk_evac)

        tk_prev_v = tk_w1qke
        tk_prev_t = tk_w1qk
        for c in range(4):
            def t_wv1_mm(e, c=c, dep=None):
                for kt in range(8):
                    e.matmul(ps[0][0:1, 0:256], o16c[:, 0:1],
                             wv16[:, kt * 1024 + c * 256:kt * 1024 + (c + 1) * 256],
                             start=(kt == 0), stop=(kt == 7))
                return None
            tk_prev_t = P.add("t", [tk_w, tk_prev_v], t_wv1_mm)

            def t_wv1_ev(e, c=c, dep_t=tk_prev_t):
                return e.tensor_copy(wv1x[0:1, c * 256:(c + 1) * 256],
                                     ps[0][0:1, 0:256])
            tk_prev_v = P.add("v", [tk_prev_t], t_wv1_ev)
        tk_wv1 = tk_prev_v

        psB = [ps[0], ps[3]]
        ST = {
            "last_x_consumer": [None, None],
            "x16_war": [None, None],
            "snap_written": [None, None],
            "snap_readers": [None, None],
            "sq_sT_war": None,
            "omm_ssq_war": [None, None],
            "sqv_war": [None, None],
            "last_pte": [None, None],
            "last_omm_pt": [None, None],
            "pte_cur": None,
            "war_vt": [None, None],
            "war_phq": None,
            "war_phk": None,
            "war_qhp": [None, None],
            "war_B": [None, None],
            "war_negmu": [None, None],
            "war_epsc": [None, None],
            "war_nm2": None,
            "war_o16": [None, None],
            "war_oT": None,
            "war_xsq": None,
            "war_fvar": None,
            "xq_war": None,
            "ybuf_war": [None, None],
            "o_evac": [],
            "oz2_prev": None,
            "upd_prev": None,
        }

        def t_pstT_init(e):
            for j in range(4):
                e.transpose(pstT[:, 512 + j * 128:512 + (j + 1) * 128],
                            id16[:, :], id16[:, :])
                e.transpose(pstT2[:, 512 + j * 128:512 + (j + 1) * 128],
                            id16[:, :], id16[:, :])
            return None
        tk_pstT_init = P.add("t", [tk_consts], t_pstT_init)
        ST["last_pte"][0] = tk_pstT_init
        ST["last_pte"][1] = tk_pstT_init
        FR = [dict() for _ in range(NSC)]
        xv = xT.rearrange("(f p) L -> p f L", p=128)
        yv = yT.rearrange("(f p) L -> p f L", p=128)
        HMAP = [(0, [0, 1, 2]), (1, [3, 4, 5]), (2, [6, 7]),
                (3, [8, 9, 10]), (4, [11, 12, 13]), (5, [14, 15])]
        spans = [SC - kb * 128 for kb in range(NSUB)]
        sso = [0]
        for kb in range(1, NSUB):
            sso.append(sso[-1] + spans[kb - 1])

        def depB(i, *toks):
            d = [t for t in toks if t is not None]
            if ST["war_B"][i] is not None:
                d.append(ST["war_B"][i])
            return d

        def front14(sc):
            par = sc % 2
            X, X16 = xs2[par], x162[par]
            F = FR[sc]

            deps = [tk_consts]
            if ST["last_x_consumer"][par] is not None:
                deps.append(ST["last_x_consumer"][par])

            def t_xload(e, sc=sc, X=X):
                e.dma_start(
                    out=X[:, :].rearrange("p (f t) -> p f t", f=NFT),
                    in_=xv[:, :, sc * SC:(sc + 1) * SC],
                ).then_inc(P.sems["qd"], 16)
            tk_x = P.dma("q", deps, t_xload, "qd", 1)
            yield

            deps16 = [tk_x, tk_zero]
            if ST["x16_war"][par] is not None:
                deps16.append(ST["x16_war"][par])

            def t_x16(e, X=X, X16=X16):
                return e.tensor_copy(X16[:, :], X[:, :])
            tk_x16 = P.add("v", deps16, t_x16)

            def t_xsq(e, X=X, par=par):
                return e.tensor_tensor(xsq2[par][:, :], X[:, :], X[:, :], ALU.mult)
            dep_xsq = [tk_x]
            if ST["war_xsq"] is not None:
                dep_xsq.append(ST["war_xsq"])
            tk_xsq = P.add("v", dep_xsq, t_xsq)
            yield

            def t_sx(e, X16=X16):
                for ft in range(NFT):
                    e.matmul(psB[0][0:1, 0:SC], o16c[:, 0:1],
                             X16[:, ft * SC:(ft + 1) * SC],
                             start=(ft == 0), stop=(ft == NFT - 1))
                return None
            tk_sx = P.add("t", depB(0, tk_x16, tk_consts), t_sx)

            def t_sxx(e, par=par):
                for ft in range(NFT):
                    e.matmul(psB[1][0:1, 0:SC], o16c[:, 0:1],
                             xsq2[par][:, ft * SC:(ft + 1) * SC],
                             start=(ft == 0), stop=(ft == NFT - 1))
                return None
            tk_sxx = P.add("t", depB(1, tk_xsq), t_sxx)
            ST["war_xsq"] = tk_sxx
            yield

            def t_stats_v(e, par=par):
                sr = srow2[par]
                P.chain(e, "v", e.tensor_scalar(sr[0:1, 0:SC], psB[0][0:1, 0:SC],
                                                -1.0 / D, None, ALU.mult))
                P.chain(e, "v", e.tensor_tensor(sr[0:1, 2 * SC:3 * SC],
                                                sr[0:1, 0:SC], sr[0:1, 0:SC],
                                                ALU.mult))
                e.scalar_tensor_tensor(sr[0:1, SC:2 * SC], psB[1][0:1, 0:SC],
                                       1.0 / D, sr[0:1, 2 * SC:3 * SC],
                                       ALU.mult, ALU.subtract)
                return e.tensor_copy(negmux2[par][0:1, :], sr[0:1, 0:SC])
            dep_stv = [tk_sx, tk_sxx]
            if ST["war_negmu"][par] is not None:
                dep_stv.append(ST["war_negmu"][par])
            tk_stv = P.add("v", dep_stv, t_stats_v)
            ST["war_B"][0] = tk_stv
            ST["war_B"][1] = tk_stv

            def t_stats_s(e, par=par):
                sr = srow2[par]
                P.chain(e, "s", e.activation(sr[0:1, 2 * SC:3 * SC],
                                             sr[0:1, SC:2 * SC], AF.Ln, bias=LN_EPS))
                e.activation(sr[0:1, 2 * SC:3 * SC], sr[0:1, 2 * SC:3 * SC],
                             AF.Exp, scale=-0.5)
                return None
            tk_sts = P.add("s", [tk_stv], t_stats_s)

            def t_stats_v2(e, par=par):
                return e.tensor_scalar(epscx2[par][0:1, :], srow2[par][0:1, SC:2 * SC],
                                       LN_EPS, 16.0 * LN_EPS, ALU.add, ALU.mult)
            dep_stv2 = [tk_sts]
            if ST["war_epsc"][par] is not None:
                dep_stv2.append(ST["war_epsc"][par])
            tk_stv2 = P.add("v", dep_stv2, t_stats_v2)

            def t_rc_t(e, par=par):
                for j in range(NSUB):
                    e.transpose(psB[0][:, 320 + j:321 + j],
                                srow2[par][0:1, 2 * SC + j * 128:2 * SC + (j + 1) * 128],
                                cstc[0:1, C_ONE:C_ONE + 1])
                return None
            tk_rct = P.add("t", depB(0, tk_sts), t_rc_t)

            def t_rc_e(e, par=par):
                return e.copy(rc2[par][:, 0:NSUB], psB[0][:, 320:320 + NSUB])
            tk_rce = P.add("s", [tk_rct], t_rc_e)
            ST["war_B"][0] = tk_rce
            yield

            def mk_proj(w16t, w1t, dst16, par=par, X16=X16):
                def t_proj(e):
                    for m in range(2):
                        for kt in range(8):
                            e.matmul(psB[m][:, 0:SC],
                                     w16t[:, kt * 256 + m * 128:kt * 256 + (m + 1) * 128],
                                     X16[:, kt * SC:kt * SC + SC],
                                     start=(kt == 0), stop=False)
                        e.matmul(psB[m][:, 0:SC],
                                 w1t[:, m * 128:(m + 1) * 128],
                                 negmux2[par][:, 0:SC],
                                 start=False, stop=True)
                    return None

                def t_evac(e):
                    e.tensor_copy(dst16[:, 0:SC], psB[0][:, 0:SC])
                    return e.tensor_copy(dst16[:, SC:2 * SC], psB[1][:, 0:SC])
                return t_proj, t_evac

            t_pq, t_pqe = mk_proj(wq16, wq1x, q162[par])
            tk_pq = P.add("t", depB(0, tk_stv, tk_w1qke, tk_sio4) + depB(1),
                          t_pq)
            ST["war_negmu"][par] = tk_pq
            tk_pqe = P.add("v", [tk_pq], t_pqe)
            ST["war_B"][0] = tk_pqe
            ST["war_B"][1] = tk_pqe
            yield

            t_pk, t_pke = mk_proj(wk16, wk1x, k162[par])
            tk_pk = P.add("t", depB(0) + depB(1), t_pk)
            ST["war_negmu"][par] = tk_pk
            tk_pke = P.add("v", [tk_pk], t_pke)
            ST["war_B"][0] = tk_pke
            ST["war_B"][1] = tk_pke
            yield

            tk_ve = None
            for tsub in range(NSUB):
                def t_pv(e, tsub=tsub, par=par, X16=X16):
                    for c in range(4):
                        reg = psB[c % 2]
                        cl = (c // 2) * 256
                        for kt in range(8):
                            e.matmul(reg[:, cl:cl + 256],
                                     X16[:, kt * SC + tsub * 128:kt * SC + (tsub + 1) * 128],
                                     wv16[:, kt * 1024 + c * 256:kt * 1024 + (c + 1) * 256],
                                     start=(kt == 0), stop=False)
                        e.matmul(reg[:, cl:cl + 256],
                                 negmux2[par][:, tsub * 128:(tsub + 1) * 128],
                                 wv1x[:, c * 256:(c + 1) * 256],
                                 start=False, stop=True)
                    return None
                tk_pv = P.add("t", depB(0, tk_rce, tk_wv1) + depB(1), t_pv)
                ST["war_negmu"][par] = tk_pv

                def t_vev(e, tsub=tsub, par=par):
                    full = vt2[par][:, tsub * H * 65:(tsub + 1) * H * 65].rearrange(
                        "p (g h cc) -> p g h cc", g=4, h=4)
                    for bk in range(2):
                        d = full[:, bk::2, :, 0:64]
                        e.tensor_scalar(
                            d, psB[bk][:, 0:512].rearrange(
                                "p (g h cc) -> p g h cc", g=2, h=4)[:, :, :, 0:64],
                            rc2[par][:, tsub:tsub + 1], None, ALU.mult)
                    return None
                dep_ve = [tk_pv]
                if tsub == 0 and ST["war_vt"][par] is not None:
                    dep_ve.append(ST["war_vt"][par])
                tk_ve = P.add("v", dep_ve, t_vev)
                ST["war_B"][0] = tk_ve
                ST["war_B"][1] = tk_ve
                yield
            F["tk_v"] = tk_ve

            def t_fsq(e, par=par):
                e.tensor_tensor(xsq2[par][:, 0:2 * SC], q162[par][:, :],
                                q162[par][:, :], ALU.mult)
                return e.tensor_tensor(xsq2[par][:, 2 * SC:4 * SC], k162[par][:, :],
                                       k162[par][:, :], ALU.mult)
            tk_fsq = P.add("v", [tk_pqe, tk_pke, tk_sxx], t_fsq)

            def t_gs1(e, par=par):
                for side, srct in ((0, q162[par]), (1, k162[par])):
                    for ftile in range(2):
                        e.matmul(psB[side][0:8, ftile * SC:(ftile + 1) * SC],
                                 onesg[:, :], srct[:, ftile * SC:(ftile + 1) * SC],
                                 start=True, stop=True)
                return None
            tk_gs1 = P.add("t", depB(0, tk_pqe, tk_pke, tk_consts) + depB(1), t_gs1)

            def t_mu(e, par=par):
                e.tensor_scalar(fmu2[par][0:8, 0:2 * SC], psB[0][0:8, 0:2 * SC],
                                1.0 / FD, None, ALU.mult)
                return e.tensor_scalar(fmu2[par][0:8, 2 * SC:4 * SC],
                                       psB[1][0:8, 0:2 * SC],
                                       1.0 / FD, None, ALU.mult)
            tk_mu = P.add("v", [tk_gs1], t_mu)
            ST["war_B"][0] = tk_mu
            ST["war_B"][1] = tk_mu
            yield

            def t_gs2(e, par=par):
                for side in range(2):
                    for ftile in range(2):
                        reg = psB[side][0:8, ftile * SC:(ftile + 1) * SC]
                        e.matmul(reg, onesg[:, :],
                                 xsq2[par][:, side * 2 * SC + ftile * SC:
                                           side * 2 * SC + (ftile + 1) * SC],
                                 start=True, stop=False)
                        e.matmul(reg, e0[:, 0:8], epscx2[par][:, 0:SC],
                                 start=False, stop=True)
                return None
            tk_gs2 = P.add("t", depB(0, tk_fsq, tk_mu, tk_stv2) + depB(1), t_gs2)
            ST["war_epsc"][par] = tk_gs2

            def t_var(e, par=par):
                P.chain(e, "v", e.tensor_tensor(flnxS[0:8, :], fmu2[par][0:8, :],
                                                fmu2[par][0:8, :], ALU.mult))
                e.scalar_tensor_tensor(fvarS[0:8, 0:2 * SC],
                                       psB[0][0:8, 0:2 * SC],
                                       1.0 / FD, flnxS[0:8, 0:2 * SC],
                                       ALU.mult, ALU.subtract)
                return e.scalar_tensor_tensor(fvarS[0:8, 2 * SC:4 * SC],
                                              psB[1][0:8, 0:2 * SC],
                                              1.0 / FD, flnxS[0:8, 2 * SC:4 * SC],
                                              ALU.mult, ALU.subtract)
            dep_var = [tk_gs2]
            if ST["war_fvar"] is not None:
                dep_var.append(ST["war_fvar"])
            tk_var = P.add("v", dep_var, t_var)
            ST["war_B"][0] = tk_var
            ST["war_B"][1] = tk_var

            def t_R_s(e, par=par):
                P.chain(e, "s", e.activation(flnxS[0:8, :], fvarS[0:8, :],
                                             AF.Ln))
                e.activation(flnxS[0:8, :], flnxS[0:8, :], AF.Exp,
                             scale=-0.5)
                return None
            tk_Rs = P.add("s", [tk_var], t_R_s)
            ST["war_fvar"] = tk_Rs

            def t_R_v(e, par=par):
                return e.tensor_copy(fR2[par][0:8, :], flnxS[0:8, :])
            tk_Rv = P.add("v", [tk_Rs], t_R_v)
            yield

            for pair in range(4):
                side, ftile = pair // 2, pair % 2
                src16 = q162[par] if side == 0 else k162[par]
                dst = qh2[par] if side == 0 else kh2[par]
                gcol = gq if side == 0 else gk
                bcol = bq if side == 0 else bk
                bi = pair % 2

                def t_bc(e, side=side, ftile=ftile, bi=bi, par=par):
                    off = side * 2 * SC + ftile * SC
                    e.matmul(psB[bi][:, 0:SC], ind8[:, :], fmu2[par][0:8, off:off + SC],
                             start=True, stop=True)
                    e.matmul(psB[bi][:, SC:2 * SC], ind8[:, :], fR2[par][0:8, off:off + SC],
                             start=True, stop=True)
                    return None
                tk_bc = P.add("t", depB(bi, tk_Rv), t_bc)

                def t_norm(e, src16=src16, dst=dst, gcol=gcol, bcol=bcol,
                           ftile=ftile, bi=bi):
                    a0 = ftile * SC
                    tp1 = t1f[:, (ftile % 2) * SC:(ftile % 2 + 1) * SC]
                    mb = psB[bi][:, 0:SC]
                    Rb = psB[bi][:, SC:2 * SC]
                    P.chain(e, "v", e.tensor_tensor(tp1, src16[:, a0:a0 + SC],
                                                    mb, ALU.subtract))
                    P.chain(e, "v", e.scalar_tensor_tensor(tp1, tp1, gcol[:, 0:1],
                                                           Rb, ALU.mult, ALU.mult))
                    return e.tensor_scalar(dst[:, a0:a0 + SC], tp1,
                                           bcol[:, 0:1], None, ALU.add)
                tk_norm = P.add("v", [tk_bc], t_norm)
                ST["war_B"][bi] = tk_norm
                if pair == 1:
                    F["tk_qh"] = tk_norm
                if pair == 3:
                    F["tk_kh"] = tk_norm
                yield

        def front5(sc):
            par = sc % 2
            F = FR[sc]

            def mk_repack(srctn, dsttn, semname):
                def t_rp(e):
                    for d, hl in HMAP:
                        for s, hh in enumerate(hl):
                            ft = hh // 8
                            p0 = (hh % 8) * 16
                            e.dma_start(
                                out=dsttn[32 * s:32 * s + 16, d * SC:(d + 1) * SC],
                                in_=srctn[p0:p0 + 16, ft * SC:(ft + 1) * SC],
                            ).then_inc(P.sems[semname], 16)
                return t_rp
            dep_rp = [F["tk_qh"], tk_zero]
            if ST["war_qhp"][par] is not None:
                dep_rp.append(ST["war_qhp"][par])
            F["tk_qhp"] = P.dma("q", dep_rp, mk_repack(qh2[par], qhp2[par], "rpq"), "rpq", 16)
            dep_rk = [F["tk_kh"], tk_zero]
            if ST["war_qhp"][par] is not None:
                dep_rk.append(ST["war_qhp"][par])
            F["tk_khp"] = P.dma("g", dep_rk, mk_repack(kh2[par], khp2[par], "rpk"), "rpk", 16)

            def mk_il(srctn, dsttn, tk_src, bank, bi):
                def t_tr(e, srctn=srctn, bank=bank):
                    for i in range(2 * NSUB):
                        tsub, ftile = i % NSUB, i // NSUB
                        e.transpose(bank[:, tsub * 256 + ftile * 128:
                                         tsub * 256 + ftile * 128 + 128],
                                    srctn[:, ftile * SC + tsub * 128:
                                          ftile * SC + (tsub + 1) * 128],
                                    pil[:, :])
                    return None
                dep = [tk_src]
                if ST["last_pte"][bi] is not None:
                    dep.append(ST["last_pte"][bi])
                tk_tr = P.add("t", dep, t_tr)

                def t_ev(e, dsttn=dsttn, bank=bank):
                    return e.tensor_copy(dsttn[:, 0:2 * 256], bank[:, 0:2 * 256])
                tk_ev = P.add("v", [tk_tr], t_ev)
                ST["last_pte"][bi] = tk_ev
                return tk_ev
            tk_qil = mk_il(qh2[par], qil, F["tk_qh"], pstT, 0)
            tk_kil = mk_il(kh2[par], kil, F["tk_kh"], pstT2, 1)

            def mk_phi_task(iltn, dsttn, tsub, factor2, war):
                def t_phi(e, tsub=tsub):
                    src = iltn[:, tsub * 256:(tsub + 1) * 256]
                    srcv = src.rearrange("p (ft a hl) -> p a ft hl", ft=2, hl=8)
                    dst = dsttn[:, tsub * PHI * 16:(tsub + 1) * PHI * 16]
                    for a in range(16):
                        nb = 16 - a
                        o = PAIR_OFF[a] * 16
                        in0 = srcv[:, a:a + 1, :, :].broadcast_to([128, nb, 2, 8])
                        in1 = srcv[:, a:16, :, :]
                        outap = dst[:, o:o + nb * 16].rearrange(
                            "p (b ft hl) -> p b ft hl", ft=2, hl=8)
                        if factor2:
                            e.scalar_tensor_tensor(outap, in0, 2.0, in1,
                                                   ALU.mult, ALU.mult)
                        else:
                            e.tensor_tensor(outap, in0, in1, ALU.mult)
                    dd = dst[:, 136 * 16:152 * 16].rearrange(
                        "p (a ft hl) -> p a ft hl", ft=2, hl=8)
                    if factor2:
                        return e.scalar_tensor_tensor(dd, srcv, -1.0, srcv,
                                                      ALU.mult, ALU.mult)
                    return e.tensor_tensor(dd, srcv, srcv, ALU.mult)
                return t_phi
            for tsub in range(NSUB):
                dep = [tk_qil]
                if tsub == 0 and ST["war_phq"] is not None:
                    dep.append(ST["war_phq"])
                F[f"tk_phq{tsub}"] = P.add("v", dep, mk_phi_task(qil, phq, tsub, False, None))
            for tsub in range(NSUB):
                dep = [tk_kil]
                if tsub == 0 and ST["war_phk"] is not None:
                    dep.append(ST["war_phk"])
                F[f"tk_phk{tsub}"] = P.add("v", dep, mk_phi_task(kil, phk, tsub, False, None))
            F["tk_phk"] = F[f"tk_phk{NSUB - 1}"]

        def attention(sc, weave=None):
            par = sc % 2
            F = FR[sc]
            snap_cur = snapa[par], snapb[par]
            snap_prev = snapa[1 - par], snapb[1 - par]
            tk_heads_done = []
            last_sT = None
            for h in range(H):
                hp = h % 2
                d_tile, hl = next((d, l) for d, l in HMAP if h in l)
                slot = hl.index(h)
                p0 = 32 * slot
                kq_t0 = d_tile * SC

                def t_sT(e, p0=p0, kq_t0=kq_t0, par=par):
                    for kb in range(NSUB):
                        span = spans[kb]
                        lhs = khp2[par][p0:p0 + 16, kq_t0 + kb * 128:kq_t0 + (kb + 1) * 128]
                        for nchunk in range(_ceil(span, MMN)):
                            c0 = nchunk * MMN
                            c1 = min(span, c0 + MMN)
                            e.matmul(ps[2][:, sso[kb] + c0:sso[kb] + c1], lhs,
                                     qhp2[par][p0:p0 + 16,
                                               kq_t0 + kb * 128 + c0:kq_t0 + kb * 128 + c1],
                                     start=True, stop=True)
                    return None
                dep_mm = [F["tk_qhp"], F["tk_khp"]]
                if ST["sq_sT_war"] is not None:
                    dep_mm.append(ST["sq_sT_war"])
                tk_mm = P.add("t", dep_mm, t_sT)
                last_sT = tk_mm

                def t_sq_s(e, hp=hp):
                    return e.activation(sq32[:, hp * SSW:(hp + 1) * SSW],
                                        ps[2][:, 0:SSW], AF.Square)
                dep_sqs = [tk_mm]
                if ST["sqv_war"][hp] is not None:
                    dep_sqs.append(ST["sqv_war"][hp])
                tk_sqs = P.add("s", dep_sqs, t_sq_s)
                ST["sq_sT_war"] = tk_sqs

                def t_sq_v(e, hp=hp):
                    return e.tensor_tensor(ssq[:, hp * SSW:(hp + 1) * SSW],
                                           sq32[:, hp * SSW:(hp + 1) * SSW],
                                           tri[:, 0:SSW], ALU.mult)
                dep_sq = [tk_sqs]
                if ST["omm_ssq_war"][hp] is not None:
                    dep_sq.append(ST["omm_ssq_war"][hp])
                tk_sq = P.add("v", dep_sq, t_sq_v)
                ST["sqv_war"][hp] = tk_sq

                if (h % 3 == 2 or h == H - 1) and sc < NSC - 1:
                    hh_rd = (h - 2, h - 1, h) if h % 3 == 2 else (h,)

                    def t_upd(e, hh_rd=hh_rd):
                        for j, hx in enumerate(hh_rd):
                            for tsub in range(NSUB):
                                lh = phk[:, tsub * PHI * 16:(tsub + 1) * PHI * 16].rearrange(
                                    "p (f hh) -> p f hh", hh=16)
                                lhsa = lh[:, 0:128, hx:hx + 1].squeeze(2)
                                rhs = vt2[par][:, tsub * H * 65 + hx * 65:
                                               tsub * H * 65 + (hx + 1) * 65]
                                e.matmul(ps7[:, j * 65:(j + 1) * 65], lhsa, rhs,
                                         start=(tsub == 0), stop=(tsub == NSUB - 1))
                        for j, hx in enumerate(hh_rd):
                            for tsub in range(NSUB):
                                lh = phk[:, tsub * PHI * 16:(tsub + 1) * PHI * 16].rearrange(
                                    "p (f hh) -> p f hh", hh=16)
                                lhsb = lh[:, 128:152, hx:hx + 1].squeeze(2)
                                rhs = vt2[par][:, tsub * H * 65 + hx * 65:
                                               tsub * H * 65 + (hx + 1) * 65]
                                e.matmul(ps7[0:24, 195 + j * 65:195 + (j + 1) * 65],
                                         lhsb, rhs,
                                         start=(tsub == 0), stop=(tsub == NSUB - 1))
                        return None
                    dep_upd = [F["tk_phk"], F["tk_v"]]
                    if ST["upd_prev"] is not None:
                        dep_upd.append(ST["upd_prev"])
                    tk_updmm = P.add("t", dep_upd, t_upd)
                    if h == H - 1:
                        ST["war_phk"] = tk_updmm

                    def t_S_add(e, hh_rd=hh_rd):
                        h0, nw = hh_rd[0], len(hh_rd) * 65
                        P.chain(e, "v", e.tensor_tensor(
                            Sa[:, h0 * 65:h0 * 65 + nw],
                            Sa[:, h0 * 65:h0 * 65 + nw],
                            ps7[:, 0:nw], ALU.add))
                        return e.tensor_tensor(
                            Sb[:, h0 * 65:h0 * 65 + nw],
                            Sb[:, h0 * 65:h0 * 65 + nw],
                            ps7[0:24, 195:195 + nw], ALU.add)
                    tk_sadd = P.add("v", [tk_updmm, tk_wv1], t_S_add)
                    ST["upd_prev"] = tk_sadd
                    tk_heads_done.append((tk_sadd, tk_sadd))

                if True:
                    tk_pte = None
                    if sc > 0 and hp == 0:
                        pb = (h // 2) % 2
                        bank_pt = pstT if pb == 0 else pstT2

                        def t_phiT(e, h=h, bank_pt=bank_pt):
                            for hx in (h, h + 1):
                                hb = (hx - h) * 512
                                for q2 in range(NSUB):
                                    sl = phq[:, q2 * PHI * 16:(q2 + 1) * PHI * 16].rearrange(
                                        "p (f hh) -> p f hh", hh=16)[:, :, hx:hx + 1].squeeze(2)
                                    e.transpose(bank_pt[:, hb + q2 * 256:hb + q2 * 256 + 128],
                                                sl[:, 0:128], id16[:, :])
                                    e.transpose(bank_pt[0:24, hb + q2 * 256 + 128:
                                                        hb + q2 * 256 + 256],
                                                sl[:, 128:152], id16[:, :])
                            return None
                        dep_pt = [F[f"tk_phq{NSUB - 1}"]]
                        if ST["last_pte"][pb] is not None:
                            dep_pt.append(ST["last_pte"][pb])
                        tk_pt = P.add("t", dep_pt, t_phiT)
                        ST["war_phq"] = tk_pt

                        def t_phiT_ev(e, bank_pt=bank_pt):
                            return e.tensor_copy(phqT[:, 0:1024], bank_pt[:, 0:1024])
                        dep_pte = [tk_pt]
                        if ST["last_omm_pt"][0] is not None:
                            dep_pte.append(ST["last_omm_pt"][0])
                        tk_pte = P.add("v", dep_pte, t_phiT_ev)
                        ST["last_pte"][pb] = tk_pte
                        ST["pte_cur"] = tk_pte

                    def t_o_mm2(e, h=h, hp=hp, sc=sc,
                                snap_prev=snap_prev, par=par):
                        reg = ps[4] if hp == 0 else ps[1]
                        for qb in range(NSUB):
                            oc = qb * 65
                            n_mm = (qb + 1) + (2 if sc > 0 else 0)
                            i = 0
                            for kb in range(qb + 1):
                                lhs = ssq[:, hp * SSW + sso[kb] + (qb - kb) * 128:
                                          hp * SSW + sso[kb] + (qb - kb) * 128 + 128]
                                rhs = vt2[par][:, kb * H * 65 + h * 65:
                                               kb * H * 65 + (h + 1) * 65]
                                e.matmul(reg[:, oc:oc + 65], lhs, rhs,
                                         start=(i == 0), stop=(i == n_mm - 1))
                                i += 1
                            if sc > 0:
                                hb = (h % 2) * 512
                                e.matmul(reg[:, oc:oc + 65],
                                         phqT[:, hb + qb * 256:hb + qb * 256 + 128],
                                         snap_prev[0][:, h * 65:(h + 1) * 65],
                                         start=False, stop=False)
                                e.matmul(reg[:, oc:oc + 65],
                                         phqT[0:24, hb + qb * 256 + 128:hb + qb * 256 + 256],
                                         snap_prev[1][:, h * 65:(h + 1) * 65],
                                         start=False, stop=True)
                        return None
                    dep_omm = [tk_sq]
                    if tk_pte is not None:
                        dep_omm.append(tk_pte)
                    if sc > 0:
                        dep_omm.append(ST["snap_written"][1 - par])
                    if len(ST["o_evac"]) >= 2:
                        dep_omm.append(ST["o_evac"][-2])
                    tk_omm = P.add("t", dep_omm, t_o_mm2)
                    ST["snap_readers"][1 - par] = tk_omm
                    ST["omm_ssq_war"][hp] = tk_omm
                    ST["last_omm_pt"][0] = tk_omm
                    if h == H - 1:
                        ST["war_vt"][par] = tk_omm
                        ST["war_qhp"][par] = last_sT

                    if True:
                        def t_oz_s(e, hp=hp):
                            reg = ps[4] if hp == 0 else ps[1]
                            zs = reg[:, 0:NSUB * 65].rearrange(
                                "p (qb c) -> p qb c", qb=NSUB)[:, :, 64:65].squeeze(2)
                            P.chain(e, "s", e.activation(
                                zsc[:, hp * NSUB:(hp + 1) * NSUB], zs, AF.Ln,
                                bias=cstc[:, C_LNEPS:C_LNEPS + 1]))
                            return e.activation(zsc[:, hp * NSUB:(hp + 1) * NSUB],
                                                zsc[:, hp * NSUB:(hp + 1) * NSUB],
                                                AF.Exp, scale=-1.0)
                        dep_oz = [tk_omm]
                        if ST["oz2_prev"] is not None:
                            dep_oz.append(ST["oz2_prev"])
                        tk_oz2 = P.add("s", dep_oz, t_oz_s)
                        ST["oz2_prev"] = tk_oz2

                        def t_oz3(e, h=h, hp=hp, par=par):
                            reg = ps[4] if hp == 0 else ps[1]
                            srco = reg[:, 0:NSUB * 65].rearrange(
                                "p (qb c) -> p qb c", qb=NSUB)[:, :, 0:64]
                            dst = o162[par][:, :].rearrange(
                                "p (qb c) -> p qb c", qb=NSUB)[:, :, h * 64:(h + 1) * 64]
                            rz = zsc[:, hp * NSUB:(hp + 1) * NSUB].unsqueeze(2).broadcast_to(
                                [128, NSUB, 64])
                            return e.tensor_tensor(dst, srco, rz, ALU.mult)
                        dep_oz3 = [tk_oz2]
                        if h < 2 and ST["war_o16"][par] is not None:
                            dep_oz3.append(ST["war_o16"][par])
                        tk_oz3 = P.add("v", dep_oz3, t_oz3)
                        ST["o_evac"].append(tk_oz3)

                if weave is not None:
                    for _ in range(4):
                        next(weave, None)
            if sc < NSC - 1:
                def t_snap_all(e, snap_cur=snap_cur):
                    e.tensor_scalar(snap_cur[0][:, 0:H * 65], Sa[:, 0:H * 65],
                                    phcoc[:, 0:1], None, ALU.mult)
                    return e.tensor_scalar(snap_cur[1][:, 0:H * 65], Sb[:, 0:H * 65],
                                           phcoc[0:24, 1:2], None, ALU.mult)
                dep_snap = [tk_heads_done[-1][0]]
                if ST["snap_readers"][par] is not None:
                    dep_snap.append(ST["snap_readers"][par])
                tk_snapall = P.add("v", dep_snap, t_snap_all)
                ST["snap_written"][par] = tk_snapall
            F["tk_attn"] = ST["o_evac"][-1]

        def tail(sc):
            par = sc % 2
            F = FR[sc]
            tk_oT = None
            for grp in range(NSUB * 2):
                bi = grp % 2

                def t_ot(e, grp=grp, par=par):
                    bank = pstT if grp % 2 == 0 else pstT2
                    for i in range(4):
                        idx = grp * 4 + i
                        qb, hd = idx % NSUB, idx // NSUB
                        e.transpose(bank[:, i * 128:(i + 1) * 128],
                                    o162[par][:, qb * 1024 + hd * 128:qb * 1024 + (hd + 1) * 128],
                                    id16[:, :])
                    return None
                deps_ot = [F["tk_attn"]]
                if ST["last_pte"][bi] is not None:
                    deps_ot.append(ST["last_pte"][bi])
                tk_ott = P.add("t", deps_ot, t_ot)
                if grp == NSUB * 2 - 1:
                    ST["war_o16"][par] = tk_ott

                def t_ote(e, grp=grp):
                    bank = pstT if grp % 2 == 0 else pstT2
                    for i in range(4):
                        idx = grp * 4 + i
                        qb, hd = idx % NSUB, idx // NSUB
                        e.tensor_copy(oT[:, hd * SC + qb * 128:hd * SC + (qb + 1) * 128],
                                      bank[:, i * 128:(i + 1) * 128])
                    return None
                dep_ote = [tk_ott]
                if tk_oT is not None:
                    dep_ote.append(tk_oT)
                if ST["war_oT"] is not None and grp == 0:
                    dep_ote.append(ST["war_oT"])
                tk_oT = P.add("v", dep_ote, t_ote)
                ST["last_pte"][bi] = tk_oT
                yield

            X = xs2[par]
            tk_res_toks = []
            last_wo = None
            for fpr in range(NFT // 2):
                def t_wo(e, fpr=fpr):
                    reg = psB[fpr % 2]
                    for j in range(2):
                        ftile = 2 * fpr + j
                        for kt in range(8):
                            e.matmul(reg[:, j * SC:(j + 1) * SC],
                                     wo16[:, kt * 1024 + ftile * 128:
                                          kt * 1024 + (ftile + 1) * 128],
                                     oT[:, kt * SC:kt * SC + SC],
                                     start=(kt == 0), stop=(kt == 7))
                    return None
                dep_wo = depB(fpr % 2, tk_oT)
                if sc == 0 and fpr < 2:
                    dep_wo.append(tk_wof)
                tk_womm = P.add("t", dep_wo, t_wo)
                last_wo = tk_womm

                def t_res(e, fpr=fpr, X=X):
                    reg = psB[fpr % 2]
                    return e.tensor_tensor(X[:, fpr * 2 * SC:(fpr + 1) * 2 * SC],
                                           X[:, fpr * 2 * SC:(fpr + 1) * 2 * SC],
                                           reg[:, 0:2 * SC], ALU.add)
                tk_res = P.add("v", [tk_womm], t_res)
                ST["war_B"][fpr % 2] = tk_res
                tk_res_toks.append(tk_res)
                yield
                yield
            ST["war_oT"] = last_wo
            tk_x2 = tk_res_toks[-1]

            def t_x2c(e, X=X, par=par):
                return e.tensor_copy(x162[par][:, :], X[:, :])
            tk_x2c = P.add("v", [tk_x2] + tk_res_toks, t_x2c)

            def t_sx2(e, par=par):
                for ft in range(NFT):
                    e.matmul(psB[0][0:1, 0:SC], o16c[:, 0:1],
                             x162[par][:, ft * SC:(ft + 1) * SC],
                             start=(ft == 0), stop=(ft == NFT - 1))
                return None
            tk_sx2 = P.add("t", depB(0, tk_x2c), t_sx2)
            ST["x16_war"][par] = tk_sx2
            yield

            def t_x2sq(e, X=X, par=par):
                return e.tensor_tensor(xsq2[par][:, :], X[:, :], X[:, :], ALU.mult)
            dep_x2sq = [tk_x2]
            if ST["war_xsq"] is not None:
                dep_x2sq.append(ST["war_xsq"])
            tk_x2sq = P.add("v", dep_x2sq, t_x2sq)

            def t_sxx2(e, par=par):
                for ft in range(NFT):
                    e.matmul(psB[1][0:1, 0:SC], o16c[:, 0:1],
                             xsq2[par][:, ft * SC:(ft + 1) * SC],
                             start=(ft == 0), stop=(ft == NFT - 1))
                return None
            tk_sxx2 = P.add("t", depB(1, tk_x2sq), t_sxx2)
            ST["war_xsq"] = tk_sxx2
            yield

            def t_l2v(e, par=par):
                sr = srow2[par]
                P.chain(e, "v", e.tensor_scalar(sr[0:1, 0:SC], psB[0][0:1, 0:SC],
                                                -1.0 / D, None, ALU.mult))
                P.chain(e, "v", e.tensor_tensor(sr[0:1, 2 * SC:3 * SC],
                                                sr[0:1, 0:SC], sr[0:1, 0:SC],
                                                ALU.mult))
                e.scalar_tensor_tensor(sr[0:1, SC:2 * SC], psB[1][0:1, 0:SC],
                                       1.0 / D, sr[0:1, 2 * SC:3 * SC],
                                       ALU.mult, ALU.subtract)
                return e.tensor_copy(nm2x[0:1, :], sr[0:1, 0:SC])
            dep_l2v = [tk_sx2, tk_sxx2]
            if ST["war_nm2"] is not None:
                dep_l2v.append(ST["war_nm2"])
            tk_l2v = P.add("v", dep_l2v, t_l2v)
            ST["war_B"][0] = tk_l2v
            ST["war_B"][1] = tk_l2v

            def t_l2s(e, par=par):
                sr = srow2[par]
                P.chain(e, "s", e.activation(sr[0:1, 2 * SC:3 * SC],
                                             sr[0:1, SC:2 * SC], AF.Ln, bias=LN_EPS))
                e.activation(sr[0:1, 2 * SC:3 * SC], sr[0:1, 2 * SC:3 * SC],
                             AF.Exp, scale=-0.5)
                return None
            tk_l2s = P.add("s", [tk_l2v], t_l2s)

            def t_l2f(e, par=par):
                return e.tensor_copy(rs2x[0:1, :], srow2[par][0:1, 2 * SC:3 * SC])
            tk_l2f = P.add("v", [tk_l2s], t_l2f)
            yield

            def t_l2bc(e):
                e.matmul(psB[0][:, 0:SC], e0[:, 0:128], nm2x[:, 0:SC],
                         start=True, stop=True)
                e.matmul(psB[1][:, 0:SC], e0[:, 0:128], rs2x[:, 0:SC],
                         start=True, stop=True)
                return None
            tk_l2bc = P.add("t", depB(0, tk_l2f) + depB(1), t_l2bc)
            ST["war_nm2"] = tk_l2bc

            tk_l2bce = tk_l2bc
            yield

            tk_q_last = None
            for fp in range(NFT // 4):
                def t_q8(e, fp=fp, X=X, par=par):
                    a0 = fp * 4 * SC
                    t2 = t1q[:, 0:4 * SC].rearrange("p (f t) -> p f t", f=4)
                    xv2 = X[:, a0:a0 + 4 * SC].rearrange("p (f t) -> p f t", f=4)
                    nmb = psB[0][:, 0:SC].unsqueeze(1).broadcast_to([128, 4, SC])
                    rsb = psB[1][:, 0:SC].unsqueeze(1).broadcast_to([128, 4, SC])
                    P.chain(e, "v", e.tensor_tensor(t2, xv2, nmb, ALU.add))
                    P.chain(e, "v", e.scalar_tensor_tensor(t2, t2, sinv[:, 0:1],
                                                           rsb, ALU.mult, ALU.mult))
                    P.chain(e, "v", e.tensor_scalar(q8f[:, 0:4 * SC], t1q[:, 0:4 * SC],
                                                    240.0, -240.0, ALU.min, ALU.max))
                    return e.tensor_copy(xq16[:, a0:a0 + 4 * SC], q8f[:, 0:4 * SC])
                dep_q8 = [tk_l2bce]
                if fp == 0 and ST.get("xq_war") is not None:
                    dep_q8.append(ST["xq_war"])
                tk_q_last = P.add("v", dep_q8, t_q8)
                ST["war_B"][0] = tk_q_last
                ST["war_B"][1] = tk_q_last
                yield
                yield
            store_toks = []
            for fpr in range(NFT // 2):
                def t_ffn(e, fpr=fpr):
                    reg = psB[fpr % 2]
                    for j in range(2):
                        ftile = 2 * fpr + j
                        for kt in range(8):
                            e.matmul(reg[:, j * SC:(j + 1) * SC],
                                     wf16[:, kt * 1024 + ftile * 128:
                                          kt * 1024 + (ftile + 1) * 128],
                                     xq16[:, kt * SC:kt * SC + SC],
                                     start=(kt == 0), stop=(kt == 7))
                    return None
                dep_ffn = depB(fpr % 2, tk_q_last)
                if sc == 0 and fpr < 2:
                    dep_ffn.append(tk_wof)
                tk_fmm = P.add("t", dep_ffn, t_ffn)
                if fpr == NFT // 2 - 1:
                    ST["xq_war"] = tk_fmm

                def t_yep(e, fpr=fpr, X=X):
                    reg = psB[fpr % 2]
                    P.chain(e, "v", e.scalar_tensor_tensor(
                        ybuf[:, 0:2 * SC], reg[:, 0:2 * SC], ssc[:, 0:1],
                        X[:, fpr * 2 * SC:(fpr + 1) * 2 * SC],
                        ALU.mult, ALU.add))
                    ret = None
                    for j in range(2):
                        ret = e.tensor_scalar(ybuf[:, j * SC:(j + 1) * SC],
                                              ybuf[:, j * SC:(j + 1) * SC],
                                              biasc[:, 2 * fpr + j:2 * fpr + j + 1],
                                              None, ALU.add)
                    return ret
                dep_yep = [tk_fmm]
                for i in range(2):
                    if ST["ybuf_war"][i] is not None:
                        dep_yep.append(ST["ybuf_war"][i])
                tk_yep = P.add("v", dep_yep, t_yep)
                ST["war_B"][fpr % 2] = tk_yep

                for j in range(2):
                    semn = "st" if j == 0 else "st1"

                    def t_store(e, ftile=2 * fpr + j, j=j, sc=sc, semn=semn):
                        e.dma_start(
                            out=yv[:, ftile, sc * SC:(sc + 1) * SC],
                            in_=ybuf[:, j * SC:(j + 1) * SC],
                        ).then_inc(P.sems[semn], 16)
                    tk_st = P.dma("q", [tk_yep], t_store, semn, 1)
                    store_toks.append(tk_st)
                    ST["ybuf_war"][j] = tk_st
                yield
                yield
            ST["last_x_consumer"][par] = store_toks[-1]

        def chain_gens(gs):
            for g in gs:
                for _ in g:
                    yield

        g0 = front14(0)
        for _ in g0:
            pass
        front5(0)
        tail_prev = None
        for sc in range(NSC):
            gens = []
            if tail_prev is not None:
                gens.append(tail_prev)
            if sc + 1 < NSC:
                gens.append(front14(sc + 1))
            weave = chain_gens(gens) if gens else None
            attention(sc, weave=weave)
            if weave is not None:
                for _ in weave:
                    pass
            tail_prev = tail(sc)
            for _ in range(4):
                next(tail_prev, None)
            if sc + 1 < NSC:
                front5(sc + 1)
        for _ in tail_prev:
            pass

        def t_final(e):
            e.wait_ge(P.sems["st"], 16 * P.dma_counts["st"])
            e.wait_ge(P.sems["st1"], 16 * P.dma_counts["st1"])
        P.tasks["q"].append(([], lambda e: (t_final(e), None)[1], ("st", 0)))

        block = ent(nc.Block())
        P.emit(block)
    return nc


def _prep_core_inputs(b, x, Wq, Wk, Wv, Wo, gamma, beta, w8, bias, s_in, s_w):
    d = {}
    d["xT"] = np.ascontiguousarray(x[b].T.astype(np.float32))
    d["wq"] = np.ascontiguousarray(Wq.T)
    d["wk"] = np.ascontiguousarray(Wk.T)
    d["wv"] = np.ascontiguousarray(Wv.T)
    d["wo"] = np.ascontiguousarray(Wo.T)
    d["wf"] = np.ascontiguousarray(w8.T)
    d["g_col"] = np.ascontiguousarray(np.tile(gamma, 8)[:, None].astype(np.float32))
    d["b_col"] = np.ascontiguousarray(np.tile(beta, 8)[:, None].astype(np.float32))
    d["bias_c"] = np.ascontiguousarray(bias.reshape(8, 128).T.astype(np.float32))
    d["s_io"] = np.array([[s_in.reshape(-1)[0], s_w.reshape(-1)[0]]], np.float32)
    _co = [2.0] * 136 + [-1.0] * 16
    _pl = np.zeros((128, 128), np.float16)
    for _hl in range(8):
        for _fd in range(16):
            _pl[_hl * 16 + _fd, _fd * 8 + _hl] = 1
    d["pil"] = _pl
    _pc = np.zeros((128, 2), np.float32)
    _pc[:, 0] = _co[0:128]
    _pc[0:24, 1] = _co[128:152]
    d["phco"] = _pc
    _tr = np.triu(np.ones((128, 128), np.float16))
    d["tri"] = np.concatenate(
        [_tr, np.ones((128, 128), np.float16), _tr], axis=1)
    d["id16"] = np.eye(128, dtype=np.float16)
    ind8 = np.zeros((8, 128), np.float16)
    for g in range(8):
        ind8[g, g * 16:(g + 1) * 16] = 1
    d["ind8"] = ind8
    onesg = np.zeros((128, 8), np.float16)
    for g in range(8):
        onesg[g * 16:(g + 1) * 16, g] = 1
    d["onesg"] = onesg
    e0 = np.zeros((128, 128), np.float16)
    e0[0, :] = 1
    d["e0"] = e0
    d["o16c"] = np.ones((128, 1), np.float16)
    d["csth"] = np.broadcast_to(
        np.array([LN_EPS, 0.25, 1.0, 0.0, -0.5, -1.0, 240.0, 16.0 * LN_EPS,
                  16.0 * LN_EPS * LN_EPS], np.float32), (128, 9)).copy()
    return d


def kernel(L0_attn_x, L0_attn_Wq, L0_attn_Wk, L0_attn_Wv, L0_attn_Wo,
           L0_attn_gamma, L0_attn_beta, L0_ffn_weight_fp8, L0_ffn_bias,
           L0_ffn_input_scale, L0_ffn_weight_scale):
    args = (np.asarray(L0_attn_x, np.float32), np.asarray(L0_attn_Wq, np.float32),
            np.asarray(L0_attn_Wk, np.float32), np.asarray(L0_attn_Wv, np.float32),
            np.asarray(L0_attn_Wo, np.float32), np.asarray(L0_attn_gamma, np.float32),
            np.asarray(L0_attn_beta, np.float32), np.asarray(L0_ffn_weight_fp8, np.float32),
            np.asarray(L0_ffn_bias, np.float32), np.asarray(L0_ffn_input_scale, np.float32),
            np.asarray(L0_ffn_weight_scale, np.float32))
    x = args[0]
    B, L, _ = x.shape
    nc = build(L=L)
    in_maps = [_prep_core_inputs(b, *args) for b in range(B)]
    res = run_bass_kernel_spmd(nc, in_maps, core_ids=list(range(B)))
    out = np.empty((B, L, D), np.float32)
    for b in range(B):
        out[b] = res.results[b]["yT"].T
    return out


# version 39
# speedup vs baseline: 1.0195x; 1.0195x over previous
import contextlib
import numpy as np

import concourse.bass as bass
import concourse.mybir as mybir
from concourse.bass_utils import run_bass_kernel_spmd

F32 = mybir.dt.float32
F16 = mybir.dt.float16
F8 = mybir.dt.float8e4
AF = mybir.ActivationFunctionType
ALU = mybir.AluOpType

D = 1024
H = 16
FD = 16
HD = 64
NFT = D // 128
PHI = 152
EPS = 1e-5
LN_EPS = 1e-5
SUB = 128
MMN = 256

PAIR_OFF = [0]
for _a in range(16):
    PAIR_OFF.append(PAIR_OFF[-1] + (16 - _a))


def _ceil(a, b):
    return (a + b - 1) // b


class Prog:

    ENGS = ("t", "v", "s", "q", "g")

    def __init__(self, nc):
        self.nc = nc
        self.tasks = {e: [] for e in self.ENGS}
        self.sems = {}
        self.counts = {e: 0 for e in self.ENGS}
        self.dma_counts = {"qd": 0, "gd": 0, "gd2": 0, "st": 0, "st1": 0,
                           "rpq": 0, "rpk": 0}

    def make_sems(self, ent):
        for e in self.ENGS:
            self.sems[e] = ent(self.nc.semaphore(f"sem_{e}"))
            self.sems["ss_" + e] = ent(self.nc.semaphore(f"ssem_{e}"))
        for d in ("qd", "gd", "gd2", "st", "st1", "rpq", "rpk"):
            self.sems[d] = ent(self.nc.semaphore(f"sem_{d}"))
        self.self_cnt = {e: 0 for e in self.ENGS}

    def chain(self, e, eng, instr):
        self.self_cnt[eng] += 1
        instr.then_inc(self.sems["ss_" + eng], 1)
        e.wait_ge(self.sems["ss_" + eng], self.self_cnt[eng])

    def add(self, eng, deps, fn):
        self.tasks[eng].append((list(deps), fn, None))
        self.counts[eng] += 1
        return (self.sems[eng], self.counts[eng])

    def dma(self, eng, deps, fn, sem_name, n=1):
        self.dma_counts[sem_name] += n
        self.tasks[eng].append((list(deps), fn, (sem_name, n)))
        return (self.sems[sem_name], 16 * self.dma_counts[sem_name])

    def emit(self, block):
        nc = self.nc

        class _Rec:
            def __init__(self, eng):
                self._e = eng
                self.last = None

            def __getattr__(self, n):
                a = getattr(self._e, n)
                if callable(a):
                    def w(*args, **kw):
                        r = a(*args, **kw)
                        if hasattr(r, "then_inc"):
                            self.last = r
                        return r
                    return w
                return a

        def mk(e):
            def body(eng):
                waited = {}
                rec = _Rec(eng)
                n_done = 0
                for deps, fn, dmainfo in self.tasks[e]:
                    if n_done > 0 and e in ("v", "s"):
                        eng.wait_ge(self.sems[e], n_done)
                    n_done += 1
                    for (sem, val) in deps:
                        if e in ("v", "s") and sem is self.sems[e]:
                            continue
                        key = id(sem)
                        if waited.get(key, -1) < val:
                            eng.wait_ge(sem, val)
                            waited[key] = val
                    rec.last = None
                    ret = fn(rec)
                    if dmainfo is None:
                        last = ret if ret is not None else rec.last
                        if last is None:
                            last = eng.nop()
                        last.then_inc(self.sems[e], 1)
            return body

        for e, attr in (("t", "tensor"), ("v", "vector"), ("s", "scalar"),
                        ("q", "sync"), ("g", "gpsimd")):
            if self.tasks[e]:
                getattr(block, attr)(mk(e))


def build(L=4096, SC=256, acc_group=16):
    NSC = L // SC
    NSUB = SC // SUB
    NNH = _ceil(SC, MMN)
    nc = bass.Bass("TRN2", target_bir_lowering=False, debug=False)

    xT = nc.dram_tensor("xT", [D, L], F32, kind="ExternalInput").ap()
    wq = nc.dram_tensor("wq", [D, 256], F32, kind="ExternalInput").ap()
    wk = nc.dram_tensor("wk", [D, 256], F32, kind="ExternalInput").ap()
    wv = nc.dram_tensor("wv", [D, D], F32, kind="ExternalInput").ap()
    wo = nc.dram_tensor("wo", [D, D], F32, kind="ExternalInput").ap()
    wf = nc.dram_tensor("wf", [D, D], F32, kind="ExternalInput").ap()
    g_col = nc.dram_tensor("g_col", [128, 1], F32, kind="ExternalInput").ap()
    b_col = nc.dram_tensor("b_col", [128, 1], F32, kind="ExternalInput").ap()
    bias_c = nc.dram_tensor("bias_c", [128, 8], F32, kind="ExternalInput").ap()
    s_io = nc.dram_tensor("s_io", [1, 2], F32, kind="ExternalInput").ap()
    tri_h = nc.dram_tensor("tri", [128, 384], F16, kind="ExternalInput").ap()
    phco_h = nc.dram_tensor("phco", [128, 2], F32, kind="ExternalInput").ap()
    pil_h = nc.dram_tensor("pil", [128, 128], F16, kind="ExternalInput").ap()
    id16_h = nc.dram_tensor("id16", [128, 128], F16, kind="ExternalInput").ap()
    ind8_h = nc.dram_tensor("ind8", [8, 128], F16, kind="ExternalInput").ap()
    onesg_h = nc.dram_tensor("onesg", [128, 8], F16, kind="ExternalInput").ap()
    e0_h = nc.dram_tensor("e0", [128, 128], F16, kind="ExternalInput").ap()
    o16c_h = nc.dram_tensor("o16c", [128, 1], F16, kind="ExternalInput").ap()
    cst_h = nc.dram_tensor("csth", [128, 9], F32, kind="ExternalInput").ap()
    yT = nc.dram_tensor("yT", [D, L], F32, kind="ExternalOutput").ap()

    with contextlib.ExitStack() as stack:
        ent = stack.enter_context
        sb = lambda name, shape, dt=F32: ent(nc.sbuf_tensor(name, shape, dt))
        psmaker = lambda name, shape, dt=F32: ent(nc.psum_tensor(name, shape, dt))

        xs2 = [sb(f"xs{i}", [128, NFT * SC]) for i in range(2)]
        x162 = [sb(f"x16_{i}", [128, NFT * SC], F16) for i in range(2)]
        wq16 = sb("wq16", [128, 8 * 256], F16)
        wk16 = sb("wk16", [128, 8 * 256], F16)
        wv16 = sb("wv16", [128, 8 * 1024], F16)
        wo16 = sb("wo16", [128, 8 * 1024], F16)
        wf16 = sb("wf16", [128, 8 * 1024], F16)
        wq1x = sb("wq1x", [128, 256], F16)
        wk1x = sb("wk1x", [128, 256], F16)
        wv1x = sb("wv1x", [128, 1024], F16)
        negmux2 = [sb(f"negmux_{i}", [128, SC], F16) for i in range(2)]
        epscx2 = [sb(f"epscx_{i}", [128, SC], F16) for i in range(2)]
        nm2x = sb("nm2x", [128, SC], F16)
        rs2x = sb("rs2x", [128, SC], F16)
        srow2 = [sb(f"srow_{i}", [1, 3 * SC]) for i in range(2)]
        rc2 = [sb(f"rc_{i}", [128, 4]) for i in range(2)]
        fmu2 = [sb(f"fmu_{i}", [8, 4 * SC], F16) for i in range(2)]
        fvarS = sb("fvar_s", [8, 4 * SC], F16)
        flnxS = sb("flnx_s", [8, 4 * SC])
        fR2 = [sb(f"fR_{i}", [8, 4 * SC], F16) for i in range(2)]
        q162 = [sb(f"q16_{i}", [128, 2 * SC], F16) for i in range(2)]
        k162 = [sb(f"k16_{i}", [128, 2 * SC], F16) for i in range(2)]
        qh2 = [sb(f"qh_{i}", [128, 2 * SC], F16) for i in range(2)]
        kh2 = [sb(f"kh_{i}", [128, 2 * SC], F16) for i in range(2)]
        qhp2 = [sb(f"qhp{i}", [128, 6 * SC], F16) for i in range(2)]
        khp2 = [sb(f"khp{i}", [128, 6 * SC], F16) for i in range(2)]
        qil = sb("qil", [128, NSUB * 256], F16)
        kil = sb("kil", [128, NSUB * 256], F16)
        phq = sb("phq", [128, NSUB * PHI * 16], F16)
        phk = sb("phk", [128, NSUB * PHI * 16], F16)
        phqT = sb("phqT", [128, 2 * 512], F16)
        vt2 = [sb(f"vt_{i}", [128, NSUB * H * 65], F16) for i in range(2)]
        SSW = sum(SC - kb * 128 for kb in range(SC // SUB))
        ssq = sb("ssq", [128, 2 * SSW], F16)
        Sa = sb("Sa", [128, H * 65])
        Sb = sb("Sb", [24, H * 65])
        snapa = [sb(f"snapa{i}", [128, H * 65], F16) for i in range(2)]
        snapb = [sb(f"snapb{i}", [24, H * 65], F16) for i in range(2)]
        o162S = sb("o16_s", [128, NSUB * 1024], F16)
        o162 = [o162S, o162S]
        xq16 = sb("xq16", [128, NFT * SC], F16)
        oT = sb("oT", [128, 8 * SC], F16)
        t1f = sb("t1f", [128, 2 * SC], F16)
        t1q = sb("t1q", [128, 4 * SC], F16)

        q8f = sb("q8f", [128, 4 * SC], F8)
        ybuf = sb("ybuf", [128, 2 * SC])
        xsq2 = [sb(f"xsq_{i}", [128, NFT * SC], F16) for i in range(2)]
        sq32 = sb("sq32", [128, 2 * SSW])
        tri = sb("tri_s", [128, 384], F16)
        phcoc = sb("phcoc", [128, 2], F32)
        pil = sb("pil_s", [128, 128], F16)
        id16 = sb("id16_s", [128, 128], F16)
        ind8 = sb("ind8_s", [8, 128], F16)
        onesg = sb("onesg_s", [128, 8], F16)
        e0 = sb("e0_s", [128, 128], F16)
        o16c = sb("o16c_s", [128, 1], F16)
        cstc = sb("cstc", [128, 9])
        gq = sb("gq", [128, 1])
        bq = sb("bq", [128, 1])
        gk = sb("gk", [128, 1])
        bk = sb("bk", [128, 1])
        biasc = sb("biasc", [128, 8])
        sioc = sb("sioc", [1, 2])
        sinv = sb("sinv", [128, 1])
        ssc = sb("ssc", [128, 1])
        zsc = sb("zsc", [128, 8])

        (C_LNEPS, C_QSC, C_ONE, C_Z, C_HALFNEG, C_NEG1, C_CLIP, C_EPS16,
         C_EPSSQ) = range(9)
        nc.const_aps.register(F32, LN_EPS, cstc[:, C_LNEPS:C_LNEPS + 1])
        nc.const_aps.register(F32, 0.25, cstc[:, C_QSC:C_QSC + 1])
        nc.const_aps.register(F32, 1.0, cstc[:, C_ONE:C_ONE + 1])
        nc.const_aps.register(F32, 0.0, cstc[:, C_Z:C_Z + 1])
        nc.const_aps.register(F32, -0.5, cstc[:, C_HALFNEG:C_HALFNEG + 1])
        nc.const_aps.register(F32, -1.0, cstc[:, C_NEG1:C_NEG1 + 1])
        nc.const_aps.register(F32, 240.0, cstc[:, C_CLIP:C_CLIP + 1])
        nc.const_aps.register(F32, 16.0 * LN_EPS, cstc[:, C_EPS16:C_EPS16 + 1])
        nc.const_aps.register(F32, 16.0 * LN_EPS * LN_EPS,
                              cstc[:, C_EPSSQ:C_EPSSQ + 1])

        ps = [psmaker(f"ps{i}", [128, 512]) for i in range(5)]
        pstT = psmaker("pstT", [128, 1024], F16)
        pstT2 = psmaker("pstT2", [128, 1024], F16)
        ps7 = psmaker("ps7", [128, 512])

        P = Prog(nc)
        P.make_sems(ent)

        def t_init_consts(e):
            e.dma_start(out=tri[:, :], in_=tri_h).then_inc(P.sems["qd"], 16)
            e.dma_start(out=id16[:, :], in_=id16_h).then_inc(P.sems["qd"], 16)
            e.dma_start(out=ind8[:, :], in_=ind8_h).then_inc(P.sems["qd"], 16)
            e.dma_start(out=onesg[:, :], in_=onesg_h).then_inc(P.sems["qd"], 16)
            e.dma_start(out=e0[:, :], in_=e0_h).then_inc(P.sems["qd"], 16)
            e.dma_start(out=o16c[:, :], in_=o16c_h).then_inc(P.sems["qd"], 16)
            e.dma_start(out=cstc[:, :], in_=cst_h).then_inc(P.sems["qd"], 16)
            e.dma_start(out=gq[:, :], in_=g_col).then_inc(P.sems["qd"], 16)
            e.dma_start(out=gk[:, :], in_=g_col).then_inc(P.sems["qd"], 16)
            e.dma_start(out=bq[:, :], in_=b_col).then_inc(P.sems["qd"], 16)
            e.dma_start(out=bk[:, :], in_=b_col).then_inc(P.sems["qd"], 16)
            e.dma_start(out=biasc[:, :], in_=bias_c).then_inc(P.sems["qd"], 16)
            e.dma_start(out=sioc[:, :], in_=s_io).then_inc(P.sems["qd"], 16)
            e.dma_start(out=phcoc[:, :], in_=phco_h).then_inc(P.sems["qd"], 16)
            e.dma_start(out=pil[:, :], in_=pil_h).then_inc(P.sems["qd"], 16)
        tk_consts = P.dma("q", [], t_init_consts, "qd", 15)

        def t_init_wqkv(e):
            for dst, srct, nf in ((wq16, wq, 256), (wk16, wk, 256),
                                  (wv16, wv, 1024)):
                e.dma_start(
                    out=dst[:, :].rearrange("p (kt f) -> p kt f", f=nf),
                    in_=srct.rearrange("(kt p) f -> p kt f", p=128),
                ).then_inc(P.sems["gd"], 16)
        tk_w = P.dma("g", [], t_init_wqkv, "gd", 3)

        def t_init_wof(e):
            for dst, srct, nf in ((wo16, wo, 1024), (wf16, wf, 1024)):
                e.dma_start(
                    out=dst[:, :].rearrange("p (kt f) -> p kt f", f=nf),
                    in_=srct.rearrange("(kt p) f -> p kt f", p=128),
                ).then_inc(P.sems["gd2"], 16)
        tk_wof = P.dma("g", [], t_init_wof, "gd2", 2)

        def t_init_zero(e):
            for i in range(2):
                e.memset(negmux2[i][:, :], 0.0)
                e.memset(epscx2[i][:, :], 0.0)
            e.memset(nm2x[:, :], 0.0)
            e.memset(rs2x[:, :], 0.0)
            e.memset(wq1x[:, :], 0.0)
            e.memset(wk1x[:, :], 0.0)
            e.memset(wv1x[:, :], 0.0)
            e.memset(Sa[:, :], 0.0)
            e.memset(Sb[:, :], 0.0)
            for i in range(2):
                e.memset(qhp2[i][:, :], 0.0)
                e.memset(khp2[i][:, :], 0.0)
            for i in range(2):
                ap = vt2[i][:, :].rearrange("p (n h c) -> p n h c",
                                            n=NSUB, h=H)[:, :, :, 64:65]
                e.memset(ap, 1.0)
            return None
        tk_zero = P.add("v", [], t_init_zero)

        def t_init_scale(e):
            e.activation(gq[:, :], gq[:, :], AF.Identity, scale=0.25)
            e.activation(bq[:, :], bq[:, :], AF.Identity, scale=0.25)
            return None
        tk_scale = P.add("s", [tk_consts], t_init_scale)

        def t_init_sio_rows(e):
            P.chain(e, "s", e.activation(srow2[0][:, 0:1], sioc[:, 0:1], AF.Ln))
            e.activation(srow2[0][:, 0:1], srow2[0][:, 0:1], AF.Exp,
                         scale=cstc[0:1, C_NEG1:C_NEG1 + 1])
            return None
        tk_sio1 = P.add("s", [tk_consts, tk_scale], t_init_sio_rows)

        def t_init_sio_mul(e):
            P.chain(e, "v", e.tensor_tensor(srow2[0][:, 1:2], sioc[:, 0:1],
                                            sioc[:, 1:2], ALU.mult))
            return e.tensor_copy(negmux2[0][0:1, 0:2], srow2[0][0:1, 0:2])
        tk_sio2 = P.add("v", [tk_sio1, tk_zero], t_init_sio_mul)

        def t_init_sio_bcast(e):
            e.matmul(ps7[:, 0:2], e0[:, 0:128], negmux2[0][:, 0:2],
                     start=True, stop=True)
            return None
        tk_sio3 = P.add("t", [tk_sio2, tk_consts], t_init_sio_bcast)

        def t_init_sio_evac(e):
            e.tensor_copy(sinv[:, :], ps7[:, 0:1])
            e.tensor_copy(ssc[:, :], ps7[:, 1:2])
            return e.memset(negmux2[0][0:1, 0:2], 0.0)
        tk_sio4 = P.add("v", [tk_sio3], t_init_sio_evac)

        def t_init_w1qk(e):
            for w16t, lo in ((wq16, 0), (wk16, 256)):
                for kt in range(8):
                    e.matmul(ps7[0:1, lo:lo + 256], o16c[:, 0:1],
                             w16t[:, kt * 256:kt * 256 + 256],
                             start=(kt == 0), stop=(kt == 7))
            return None
        tk_w1qk = P.add("t", [tk_w, tk_consts, tk_sio4], t_init_w1qk)

        def t_init_w1qk_evac(e):
            e.tensor_copy(wq1x[0:1, :], ps7[0:1, 0:256])
            return e.tensor_copy(wk1x[0:1, :], ps7[0:1, 256:512])
        tk_w1qke = P.add("v", [tk_w1qk, tk_zero, tk_sio4], t_init_w1qk_evac)

        tk_prev_v = tk_w1qke
        tk_prev_t = tk_w1qk
        for c in range(4):
            def t_wv1_mm(e, c=c, dep=None):
                for kt in range(8):
                    e.matmul(ps[0][0:1, 0:256], o16c[:, 0:1],
                             wv16[:, kt * 1024 + c * 256:kt * 1024 + (c + 1) * 256],
                             start=(kt == 0), stop=(kt == 7))
                return None
            tk_prev_t = P.add("t", [tk_w, tk_prev_v], t_wv1_mm)

            def t_wv1_ev(e, c=c, dep_t=tk_prev_t):
                return e.tensor_copy(wv1x[0:1, c * 256:(c + 1) * 256],
                                     ps[0][0:1, 0:256])
            tk_prev_v = P.add("v", [tk_prev_t], t_wv1_ev)
        tk_wv1 = tk_prev_v

        psB = [ps[0], ps[3]]
        ST = {
            "last_x_consumer": [None, None],
            "x16_war": [None, None],
            "snap_written": [None, None],
            "snap_readers": [None, None],
            "sq_sT_war": None,
            "omm_ssq_war": [None, None],
            "sqv_war": [None, None],
            "last_pte": [None, None],
            "last_omm_pt": [None, None],
            "pte_cur": None,
            "war_vt": [None, None],
            "war_phq": None,
            "war_phk": None,
            "war_qhp": [None, None],
            "war_B": [None, None],
            "war_negmu": [None, None],
            "war_epsc": [None, None],
            "war_nm2": None,
            "war_o16": [None, None],
            "war_oT": None,
            "war_xsq": None,
            "war_fvar": None,
            "xq_war": None,
            "ybuf_war": [None, None],
            "o_evac": [],
            "oz2_prev": None,
            "upd_prev": None,
        }

        def t_pstT_init(e):
            for j in range(4):
                e.transpose(pstT[:, 512 + j * 128:512 + (j + 1) * 128],
                            id16[:, :], id16[:, :])
                e.transpose(pstT2[:, 512 + j * 128:512 + (j + 1) * 128],
                            id16[:, :], id16[:, :])
            return None
        tk_pstT_init = P.add("t", [tk_consts], t_pstT_init)
        ST["last_pte"][0] = tk_pstT_init
        ST["last_pte"][1] = tk_pstT_init
        FR = [dict() for _ in range(NSC)]
        xv = xT.rearrange("(f p) L -> p f L", p=128)
        yv = yT.rearrange("(f p) L -> p f L", p=128)
        HMAP = [(0, [0, 1, 2]), (1, [3, 4, 5]), (2, [6, 7]),
                (3, [8, 9, 10]), (4, [11, 12, 13]), (5, [14, 15])]
        spans = [SC - kb * 128 for kb in range(NSUB)]
        sso = [0]
        for kb in range(1, NSUB):
            sso.append(sso[-1] + spans[kb - 1])

        def depB(i, *toks):
            d = [t for t in toks if t is not None]
            if ST["war_B"][i] is not None:
                d.append(ST["war_B"][i])
            return d

        def front14(sc):
            par = sc % 2
            X, X16 = xs2[par], x162[par]
            F = FR[sc]

            deps = [tk_consts]
            if ST["last_x_consumer"][par] is not None:
                deps.append(ST["last_x_consumer"][par])

            def t_xload(e, sc=sc, X=X):
                e.dma_start(
                    out=X[:, :].rearrange("p (f t) -> p f t", f=NFT),
                    in_=xv[:, :, sc * SC:(sc + 1) * SC],
                ).then_inc(P.sems["qd"], 16)
            tk_x = P.dma("q", deps, t_xload, "qd", 1)
            yield

            deps16 = [tk_x, tk_zero]
            if ST["x16_war"][par] is not None:
                deps16.append(ST["x16_war"][par])

            def t_x16(e, X=X, X16=X16):
                return e.tensor_copy(X16[:, :], X[:, :])
            tk_x16 = P.add("v", deps16, t_x16)

            def t_xsq(e, X=X, par=par):
                return e.tensor_tensor(xsq2[par][:, :], X[:, :], X[:, :], ALU.mult)
            dep_xsq = [tk_x]
            if ST["war_xsq"] is not None:
                dep_xsq.append(ST["war_xsq"])
            tk_xsq = P.add("v", dep_xsq, t_xsq)
            yield

            def t_sx(e, X16=X16):
                for ft in range(NFT):
                    e.matmul(psB[0][0:1, 0:SC], o16c[:, 0:1],
                             X16[:, ft * SC:(ft + 1) * SC],
                             start=(ft == 0), stop=(ft == NFT - 1))
                return None
            tk_sx = P.add("t", depB(0, tk_x16, tk_consts), t_sx)

            def t_sxx(e, par=par):
                for ft in range(NFT):
                    e.matmul(psB[1][0:1, 0:SC], o16c[:, 0:1],
                             xsq2[par][:, ft * SC:(ft + 1) * SC],
                             start=(ft == 0), stop=(ft == NFT - 1))
                return None
            tk_sxx = P.add("t", depB(1, tk_xsq), t_sxx)
            ST["war_xsq"] = tk_sxx
            yield

            def t_stats_v(e, par=par):
                sr = srow2[par]
                P.chain(e, "v", e.tensor_scalar(sr[0:1, 0:SC], psB[0][0:1, 0:SC],
                                                -1.0 / D, None, ALU.mult))
                P.chain(e, "v", e.tensor_tensor(sr[0:1, 2 * SC:3 * SC],
                                                sr[0:1, 0:SC], sr[0:1, 0:SC],
                                                ALU.mult))
                e.scalar_tensor_tensor(sr[0:1, SC:2 * SC], psB[1][0:1, 0:SC],
                                       1.0 / D, sr[0:1, 2 * SC:3 * SC],
                                       ALU.mult, ALU.subtract)
                return e.tensor_copy(negmux2[par][0:1, :], sr[0:1, 0:SC])
            dep_stv = [tk_sx, tk_sxx]
            if ST["war_negmu"][par] is not None:
                dep_stv.append(ST["war_negmu"][par])
            tk_stv = P.add("v", dep_stv, t_stats_v)
            ST["war_B"][0] = tk_stv
            ST["war_B"][1] = tk_stv

            def t_stats_s(e, par=par):
                sr = srow2[par]
                P.chain(e, "s", e.activation(sr[0:1, 2 * SC:3 * SC],
                                             sr[0:1, SC:2 * SC], AF.Ln, bias=LN_EPS))
                e.activation(sr[0:1, 2 * SC:3 * SC], sr[0:1, 2 * SC:3 * SC],
                             AF.Exp, scale=-0.5)
                return None
            tk_sts = P.add("s", [tk_stv], t_stats_s)

            def t_stats_v2(e, par=par):
                return e.tensor_scalar(epscx2[par][0:1, :], srow2[par][0:1, SC:2 * SC],
                                       LN_EPS, 16.0 * LN_EPS, ALU.add, ALU.mult)
            dep_stv2 = [tk_sts]
            if ST["war_epsc"][par] is not None:
                dep_stv2.append(ST["war_epsc"][par])
            tk_stv2 = P.add("v", dep_stv2, t_stats_v2)

            def t_rc_t(e, par=par):
                for j in range(NSUB):
                    e.transpose(psB[0][:, 320 + j:321 + j],
                                srow2[par][0:1, 2 * SC + j * 128:2 * SC + (j + 1) * 128],
                                cstc[0:1, C_ONE:C_ONE + 1])
                return None
            tk_rct = P.add("t", depB(0, tk_sts), t_rc_t)

            def t_rc_e(e, par=par):
                return e.copy(rc2[par][:, 0:NSUB], psB[0][:, 320:320 + NSUB])
            tk_rce = P.add("s", [tk_rct], t_rc_e)
            ST["war_B"][0] = tk_rce
            yield

            def mk_proj(w16t, w1t, dst16, par=par, X16=X16):
                def t_proj(e):
                    for m in range(2):
                        for kt in range(8):
                            e.matmul(psB[m][:, 0:SC],
                                     w16t[:, kt * 256 + m * 128:kt * 256 + (m + 1) * 128],
                                     X16[:, kt * SC:kt * SC + SC],
                                     start=(kt == 0), stop=False)
                        e.matmul(psB[m][:, 0:SC],
                                 w1t[:, m * 128:(m + 1) * 128],
                                 negmux2[par][:, 0:SC],
                                 start=False, stop=True)
                    return None

                def t_evac(e):
                    e.tensor_copy(dst16[:, 0:SC], psB[0][:, 0:SC])
                    return e.tensor_copy(dst16[:, SC:2 * SC], psB[1][:, 0:SC])
                return t_proj, t_evac

            t_pq, t_pqe = mk_proj(wq16, wq1x, q162[par])
            tk_pq = P.add("t", depB(0, tk_stv, tk_w1qke, tk_sio4) + depB(1),
                          t_pq)
            ST["war_negmu"][par] = tk_pq
            tk_pqe = P.add("v", [tk_pq], t_pqe)
            ST["war_B"][0] = tk_pqe
            ST["war_B"][1] = tk_pqe
            yield

            t_pk, t_pke = mk_proj(wk16, wk1x, k162[par])
            tk_pk = P.add("t", depB(0) + depB(1), t_pk)
            ST["war_negmu"][par] = tk_pk
            tk_pke = P.add("v", [tk_pk], t_pke)
            ST["war_B"][0] = tk_pke
            ST["war_B"][1] = tk_pke
            yield

            tk_ve = None
            for tsub in range(NSUB):
                def t_pv(e, tsub=tsub, par=par, X16=X16):
                    for c in range(4):
                        reg = psB[c % 2]
                        cl = (c // 2) * 256
                        for kt in range(8):
                            e.matmul(reg[:, cl:cl + 256],
                                     X16[:, kt * SC + tsub * 128:kt * SC + (tsub + 1) * 128],
                                     wv16[:, kt * 1024 + c * 256:kt * 1024 + (c + 1) * 256],
                                     start=(kt == 0), stop=False)
                        e.matmul(reg[:, cl:cl + 256],
                                 negmux2[par][:, tsub * 128:(tsub + 1) * 128],
                                 wv1x[:, c * 256:(c + 1) * 256],
                                 start=False, stop=True)
                    return None
                tk_pv = P.add("t", depB(0, tk_rce, tk_wv1) + depB(1), t_pv)
                ST["war_negmu"][par] = tk_pv

                def t_vev(e, tsub=tsub, par=par):
                    full = vt2[par][:, tsub * H * 65:(tsub + 1) * H * 65].rearrange(
                        "p (g h cc) -> p g h cc", g=4, h=4)
                    for bk in range(2):
                        d = full[:, bk::2, :, 0:64]
                        e.tensor_scalar(
                            d, psB[bk][:, 0:512].rearrange(
                                "p (g h cc) -> p g h cc", g=2, h=4)[:, :, :, 0:64],
                            rc2[par][:, tsub:tsub + 1], None, ALU.mult)
                    return None
                dep_ve = [tk_pv]
                if tsub == 0 and ST["war_vt"][par] is not None:
                    dep_ve.append(ST["war_vt"][par])
                tk_ve = P.add("v", dep_ve, t_vev)
                ST["war_B"][0] = tk_ve
                ST["war_B"][1] = tk_ve
                yield
            F["tk_v"] = tk_ve

            def t_fsq(e, par=par):
                e.tensor_tensor(xsq2[par][:, 0:2 * SC], q162[par][:, :],
                                q162[par][:, :], ALU.mult)
                return e.tensor_tensor(xsq2[par][:, 2 * SC:4 * SC], k162[par][:, :],
                                       k162[par][:, :], ALU.mult)
            tk_fsq = P.add("v", [tk_pqe, tk_pke, tk_sxx], t_fsq)

            def t_gs1(e, par=par):
                for side, srct in ((0, q162[par]), (1, k162[par])):
                    for ftile in range(2):
                        e.matmul(psB[side][0:8, ftile * SC:(ftile + 1) * SC],
                                 onesg[:, :], srct[:, ftile * SC:(ftile + 1) * SC],
                                 start=True, stop=True)
                return None
            tk_gs1 = P.add("t", depB(0, tk_pqe, tk_pke, tk_consts) + depB(1), t_gs1)

            def t_mu(e, par=par):
                e.tensor_scalar(fmu2[par][0:8, 0:2 * SC], psB[0][0:8, 0:2 * SC],
                                1.0 / FD, None, ALU.mult)
                return e.tensor_scalar(fmu2[par][0:8, 2 * SC:4 * SC],
                                       psB[1][0:8, 0:2 * SC],
                                       1.0 / FD, None, ALU.mult)
            tk_mu = P.add("v", [tk_gs1], t_mu)
            ST["war_B"][0] = tk_mu
            ST["war_B"][1] = tk_mu
            yield

            def t_gs2(e, par=par):
                for side in range(2):
                    for ftile in range(2):
                        reg = psB[side][0:8, ftile * SC:(ftile + 1) * SC]
                        e.matmul(reg, onesg[:, :],
                                 xsq2[par][:, side * 2 * SC + ftile * SC:
                                           side * 2 * SC + (ftile + 1) * SC],
                                 start=True, stop=False)
                        e.matmul(reg, e0[:, 0:8], epscx2[par][:, 0:SC],
                                 start=False, stop=True)
                return None
            tk_gs2 = P.add("t", depB(0, tk_fsq, tk_mu, tk_stv2) + depB(1), t_gs2)
            ST["war_epsc"][par] = tk_gs2

            def t_var(e, par=par):
                P.chain(e, "v", e.tensor_tensor(flnxS[0:8, :], fmu2[par][0:8, :],
                                                fmu2[par][0:8, :], ALU.mult))
                e.scalar_tensor_tensor(fvarS[0:8, 0:2 * SC],
                                       psB[0][0:8, 0:2 * SC],
                                       1.0 / FD, flnxS[0:8, 0:2 * SC],
                                       ALU.mult, ALU.subtract)
                return e.scalar_tensor_tensor(fvarS[0:8, 2 * SC:4 * SC],
                                              psB[1][0:8, 0:2 * SC],
                                              1.0 / FD, flnxS[0:8, 2 * SC:4 * SC],
                                              ALU.mult, ALU.subtract)
            dep_var = [tk_gs2]
            if ST["war_fvar"] is not None:
                dep_var.append(ST["war_fvar"])
            tk_var = P.add("v", dep_var, t_var)
            ST["war_B"][0] = tk_var
            ST["war_B"][1] = tk_var

            def t_R_s(e, par=par):
                P.chain(e, "s", e.activation(flnxS[0:8, :], fvarS[0:8, :],
                                             AF.Ln))
                e.activation(flnxS[0:8, :], flnxS[0:8, :], AF.Exp,
                             scale=-0.5)
                return None
            tk_Rs = P.add("s", [tk_var], t_R_s)
            ST["war_fvar"] = tk_Rs

            def t_R_v(e, par=par):
                return e.tensor_copy(fR2[par][0:8, :], flnxS[0:8, :])
            tk_Rv = P.add("v", [tk_Rs], t_R_v)
            yield

            for pair in range(4):
                side, ftile = pair // 2, pair % 2
                src16 = q162[par] if side == 0 else k162[par]
                dst = qh2[par] if side == 0 else kh2[par]
                gcol = gq if side == 0 else gk
                bcol = bq if side == 0 else bk
                bi = pair % 2

                def t_bc(e, side=side, ftile=ftile, bi=bi, par=par):
                    off = side * 2 * SC + ftile * SC
                    e.matmul(psB[bi][:, 0:SC], ind8[:, :], fmu2[par][0:8, off:off + SC],
                             start=True, stop=True)
                    e.matmul(psB[bi][:, SC:2 * SC], ind8[:, :], fR2[par][0:8, off:off + SC],
                             start=True, stop=True)
                    return None
                tk_bc = P.add("t", depB(bi, tk_Rv), t_bc)

                def t_norm(e, src16=src16, dst=dst, gcol=gcol, bcol=bcol,
                           ftile=ftile, bi=bi):
                    a0 = ftile * SC
                    tp1 = t1f[:, (ftile % 2) * SC:(ftile % 2 + 1) * SC]
                    mb = psB[bi][:, 0:SC]
                    Rb = psB[bi][:, SC:2 * SC]
                    P.chain(e, "v", e.tensor_tensor(tp1, src16[:, a0:a0 + SC],
                                                    mb, ALU.subtract))
                    P.chain(e, "v", e.scalar_tensor_tensor(tp1, tp1, gcol[:, 0:1],
                                                           Rb, ALU.mult, ALU.mult))
                    return e.tensor_scalar(dst[:, a0:a0 + SC], tp1,
                                           bcol[:, 0:1], None, ALU.add)
                tk_norm = P.add("v", [tk_bc], t_norm)
                ST["war_B"][bi] = tk_norm
                if pair == 1:
                    F["tk_qh"] = tk_norm
                if pair == 3:
                    F["tk_kh"] = tk_norm
                yield

        def front5(sc):
            par = sc % 2
            F = FR[sc]

            def mk_repack(srctn, dsttn, semname):
                def t_rp(e):
                    for d, hl in HMAP:
                        for s, hh in enumerate(hl):
                            ft = hh // 8
                            p0 = (hh % 8) * 16
                            e.dma_start(
                                out=dsttn[32 * s:32 * s + 16, d * SC:(d + 1) * SC],
                                in_=srctn[p0:p0 + 16, ft * SC:(ft + 1) * SC],
                            ).then_inc(P.sems[semname], 16)
                return t_rp
            dep_rp = [F["tk_qh"], tk_zero]
            if ST["war_qhp"][par] is not None:
                dep_rp.append(ST["war_qhp"][par])
            F["tk_qhp"] = P.dma("q", dep_rp, mk_repack(qh2[par], qhp2[par], "rpq"), "rpq", 16)
            dep_rk = [F["tk_kh"], tk_zero]
            if ST["war_qhp"][par] is not None:
                dep_rk.append(ST["war_qhp"][par])
            F["tk_khp"] = P.dma("g", dep_rk, mk_repack(kh2[par], khp2[par], "rpk"), "rpk", 16)

            def mk_il(srctn, dsttn, tk_src, bank, bi):
                def t_tr(e, srctn=srctn, bank=bank):
                    for i in range(2 * NSUB):
                        tsub, ftile = i % NSUB, i // NSUB
                        e.transpose(bank[:, tsub * 256 + ftile * 128:
                                         tsub * 256 + ftile * 128 + 128],
                                    srctn[:, ftile * SC + tsub * 128:
                                          ftile * SC + (tsub + 1) * 128],
                                    pil[:, :])
                    return None
                dep = [tk_src]
                if ST["last_pte"][bi] is not None:
                    dep.append(ST["last_pte"][bi])
                tk_tr = P.add("t", dep, t_tr)

                def t_ev(e, dsttn=dsttn, bank=bank):
                    return e.tensor_copy(dsttn[:, 0:2 * 256], bank[:, 0:2 * 256])
                tk_ev = P.add("v", [tk_tr], t_ev)
                ST["last_pte"][bi] = tk_ev
                return tk_ev
            tk_qil = mk_il(qh2[par], qil, F["tk_qh"], pstT, 0)
            tk_kil = mk_il(kh2[par], kil, F["tk_kh"], pstT2, 1)

            def mk_phi_task(iltn, dsttn, tsub, factor2, war):
                def t_phi(e, tsub=tsub):
                    src = iltn[:, tsub * 256:(tsub + 1) * 256]
                    srcv = src.rearrange("p (ft a hl) -> p a ft hl", ft=2, hl=8)
                    dst = dsttn[:, tsub * PHI * 16:(tsub + 1) * PHI * 16]
                    for a in range(16):
                        nb = 16 - a
                        o = PAIR_OFF[a] * 16
                        in0 = srcv[:, a:a + 1, :, :].broadcast_to([128, nb, 2, 8])
                        in1 = srcv[:, a:16, :, :]
                        outap = dst[:, o:o + nb * 16].rearrange(
                            "p (b ft hl) -> p b ft hl", ft=2, hl=8)
                        if factor2:
                            e.scalar_tensor_tensor(outap, in0, 2.0, in1,
                                                   ALU.mult, ALU.mult)
                        else:
                            e.tensor_tensor(outap, in0, in1, ALU.mult)
                    dd = dst[:, 136 * 16:152 * 16].rearrange(
                        "p (a ft hl) -> p a ft hl", ft=2, hl=8)
                    if factor2:
                        return e.scalar_tensor_tensor(dd, srcv, -1.0, srcv,
                                                      ALU.mult, ALU.mult)
                    return e.tensor_tensor(dd, srcv, srcv, ALU.mult)
                return t_phi
            for tsub in range(NSUB):
                dep = [tk_qil]
                if tsub == 0 and ST["war_phq"] is not None:
                    dep.append(ST["war_phq"])
                F[f"tk_phq{tsub}"] = P.add("v", dep, mk_phi_task(qil, phq, tsub, False, None))
            for tsub in range(NSUB):
                dep = [tk_kil]
                if tsub == 0 and ST["war_phk"] is not None:
                    dep.append(ST["war_phk"])
                F[f"tk_phk{tsub}"] = P.add("v", dep, mk_phi_task(kil, phk, tsub, False, None))
            F["tk_phk"] = F[f"tk_phk{NSUB - 1}"]

        def attention(sc, weave=None):
            par = sc % 2
            F = FR[sc]
            snap_cur = snapa[par], snapb[par]
            snap_prev = snapa[1 - par], snapb[1 - par]
            tk_heads_done = []
            last_sT = None
            for h in range(H):
                hp = h % 2
                d_tile, hl = next((d, l) for d, l in HMAP if h in l)
                slot = hl.index(h)
                p0 = 32 * slot
                kq_t0 = d_tile * SC

                def t_sT(e, p0=p0, kq_t0=kq_t0, par=par):
                    for kb in range(NSUB):
                        span = spans[kb]
                        lhs = khp2[par][p0:p0 + 16, kq_t0 + kb * 128:kq_t0 + (kb + 1) * 128]
                        for nchunk in range(_ceil(span, MMN)):
                            c0 = nchunk * MMN
                            c1 = min(span, c0 + MMN)
                            e.matmul(ps[2][:, sso[kb] + c0:sso[kb] + c1], lhs,
                                     qhp2[par][p0:p0 + 16,
                                               kq_t0 + kb * 128 + c0:kq_t0 + kb * 128 + c1],
                                     start=True, stop=True)
                    return None
                dep_mm = [F["tk_qhp"], F["tk_khp"]]
                if ST["sq_sT_war"] is not None:
                    dep_mm.append(ST["sq_sT_war"])
                tk_mm = P.add("t", dep_mm, t_sT)
                last_sT = tk_mm

                def t_sq_s(e, hp=hp):
                    return e.activation(sq32[:, hp * SSW:(hp + 1) * SSW],
                                        ps[2][:, 0:SSW], AF.Square)
                dep_sqs = [tk_mm]
                if ST["sqv_war"][hp] is not None:
                    dep_sqs.append(ST["sqv_war"][hp])
                tk_sqs = P.add("s", dep_sqs, t_sq_s)
                ST["sq_sT_war"] = tk_sqs

                def t_sq_v(e, hp=hp):
                    return e.tensor_tensor(ssq[:, hp * SSW:(hp + 1) * SSW],
                                           sq32[:, hp * SSW:(hp + 1) * SSW],
                                           tri[:, 0:SSW], ALU.mult)
                dep_sq = [tk_sqs]
                if ST["omm_ssq_war"][hp] is not None:
                    dep_sq.append(ST["omm_ssq_war"][hp])
                tk_sq = P.add("v", dep_sq, t_sq_v)
                ST["sqv_war"][hp] = tk_sq

                if (h % 3 == 2 or h == H - 1) and sc < NSC - 1:
                    hh_rd = (h - 2, h - 1, h) if h % 3 == 2 else (h,)

                    def t_upd(e, hh_rd=hh_rd):
                        for j, hx in enumerate(hh_rd):
                            for tsub in range(NSUB):
                                lh = phk[:, tsub * PHI * 16:(tsub + 1) * PHI * 16].rearrange(
                                    "p (f hh) -> p f hh", hh=16)
                                lhsa = lh[:, 0:128, hx:hx + 1].squeeze(2)
                                rhs = vt2[par][:, tsub * H * 65 + hx * 65:
                                               tsub * H * 65 + (hx + 1) * 65]
                                e.matmul(ps7[:, j * 65:(j + 1) * 65], lhsa, rhs,
                                         start=(tsub == 0), stop=(tsub == NSUB - 1))
                        for j, hx in enumerate(hh_rd):
                            for tsub in range(NSUB):
                                lh = phk[:, tsub * PHI * 16:(tsub + 1) * PHI * 16].rearrange(
                                    "p (f hh) -> p f hh", hh=16)
                                lhsb = lh[:, 128:152, hx:hx + 1].squeeze(2)
                                rhs = vt2[par][:, tsub * H * 65 + hx * 65:
                                               tsub * H * 65 + (hx + 1) * 65]
                                e.matmul(ps7[0:24, 195 + j * 65:195 + (j + 1) * 65],
                                         lhsb, rhs,
                                         start=(tsub == 0), stop=(tsub == NSUB - 1))
                        return None
                    dep_upd = [F["tk_phk"], F["tk_v"]]
                    if ST["upd_prev"] is not None:
                        dep_upd.append(ST["upd_prev"])
                    tk_updmm = P.add("t", dep_upd, t_upd)
                    if h == H - 1:
                        ST["war_phk"] = tk_updmm

                    def t_S_add(e, hh_rd=hh_rd):
                        h0, nw = hh_rd[0], len(hh_rd) * 65
                        P.chain(e, "v", e.tensor_tensor(
                            Sa[:, h0 * 65:h0 * 65 + nw],
                            Sa[:, h0 * 65:h0 * 65 + nw],
                            ps7[:, 0:nw], ALU.add))
                        return e.tensor_tensor(
                            Sb[:, h0 * 65:h0 * 65 + nw],
                            Sb[:, h0 * 65:h0 * 65 + nw],
                            ps7[0:24, 195:195 + nw], ALU.add)
                    tk_sadd = P.add("v", [tk_updmm, tk_wv1], t_S_add)
                    ST["upd_prev"] = tk_sadd
                    tk_heads_done.append((tk_sadd, tk_sadd))

                if True:
                    tk_pte = None
                    if sc > 0 and hp == 0:
                        pb = (h // 2) % 2
                        bank_pt = pstT if pb == 0 else pstT2

                        def t_phiT(e, h=h, bank_pt=bank_pt):
                            for hx in (h, h + 1):
                                hb = (hx - h) * 512
                                for q2 in range(NSUB):
                                    sl = phq[:, q2 * PHI * 16:(q2 + 1) * PHI * 16].rearrange(
                                        "p (f hh) -> p f hh", hh=16)[:, :, hx:hx + 1].squeeze(2)
                                    e.transpose(bank_pt[:, hb + q2 * 256:hb + q2 * 256 + 128],
                                                sl[:, 0:128], id16[:, :])
                                    e.transpose(bank_pt[0:24, hb + q2 * 256 + 128:
                                                        hb + q2 * 256 + 256],
                                                sl[:, 128:152], id16[:, :])
                            return None
                        dep_pt = [F[f"tk_phq{NSUB - 1}"]]
                        if ST["last_pte"][pb] is not None:
                            dep_pt.append(ST["last_pte"][pb])
                        tk_pt = P.add("t", dep_pt, t_phiT)
                        ST["war_phq"] = tk_pt

                        def t_phiT_ev(e, bank_pt=bank_pt):
                            return e.tensor_copy(phqT[:, 0:1024], bank_pt[:, 0:1024])
                        dep_pte = [tk_pt]
                        if ST["last_omm_pt"][0] is not None:
                            dep_pte.append(ST["last_omm_pt"][0])
                        tk_pte = P.add("v", dep_pte, t_phiT_ev)
                        ST["last_pte"][pb] = tk_pte
                        ST["pte_cur"] = tk_pte

                    def t_o_mm2(e, h=h, hp=hp, sc=sc,
                                snap_prev=snap_prev, par=par):
                        reg = ps[4] if hp == 0 else ps[1]
                        for qb in range(NSUB):
                            oc = qb * 65
                            n_mm = (qb + 1) + (2 if sc > 0 else 0)
                            i = 0
                            for kb in range(qb + 1):
                                lhs = ssq[:, hp * SSW + sso[kb] + (qb - kb) * 128:
                                          hp * SSW + sso[kb] + (qb - kb) * 128 + 128]
                                rhs = vt2[par][:, kb * H * 65 + h * 65:
                                               kb * H * 65 + (h + 1) * 65]
                                e.matmul(reg[:, oc:oc + 65], lhs, rhs,
                                         start=(i == 0), stop=(i == n_mm - 1))
                                i += 1
                            if sc > 0:
                                hb = (h % 2) * 512
                                e.matmul(reg[:, oc:oc + 65],
                                         phqT[:, hb + qb * 256:hb + qb * 256 + 128],
                                         snap_prev[0][:, h * 65:(h + 1) * 65],
                                         start=False, stop=False)
                                e.matmul(reg[:, oc:oc + 65],
                                         phqT[0:24, hb + qb * 256 + 128:hb + qb * 256 + 256],
                                         snap_prev[1][:, h * 65:(h + 1) * 65],
                                         start=False, stop=True)
                        return None
                    dep_omm = [tk_sq]
                    if tk_pte is not None:
                        dep_omm.append(tk_pte)
                    if sc > 0:
                        dep_omm.append(ST["snap_written"][1 - par])
                    if len(ST["o_evac"]) >= 2:
                        dep_omm.append(ST["o_evac"][-2])
                    tk_omm = P.add("t", dep_omm, t_o_mm2)
                    ST["snap_readers"][1 - par] = tk_omm
                    ST["omm_ssq_war"][hp] = tk_omm
                    ST["last_omm_pt"][0] = tk_omm
                    if h == H - 1:
                        ST["war_vt"][par] = tk_omm
                        ST["war_qhp"][par] = last_sT

                    if True:
                        def t_oz_s(e, hp=hp):
                            reg = ps[4] if hp == 0 else ps[1]
                            zs = reg[:, 0:NSUB * 65].rearrange(
                                "p (qb c) -> p qb c", qb=NSUB)[:, :, 64:65].squeeze(2)
                            P.chain(e, "s", e.activation(
                                zsc[:, hp * NSUB:(hp + 1) * NSUB], zs, AF.Ln,
                                bias=cstc[:, C_LNEPS:C_LNEPS + 1]))
                            return e.activation(zsc[:, hp * NSUB:(hp + 1) * NSUB],
                                                zsc[:, hp * NSUB:(hp + 1) * NSUB],
                                                AF.Exp, scale=-1.0)
                        dep_oz = [tk_omm]
                        if ST["oz2_prev"] is not None:
                            dep_oz.append(ST["oz2_prev"])
                        tk_oz2 = P.add("s", dep_oz, t_oz_s)
                        ST["oz2_prev"] = tk_oz2

                        def t_oz3(e, h=h, hp=hp, par=par):
                            reg = ps[4] if hp == 0 else ps[1]
                            srco = reg[:, 0:NSUB * 65].rearrange(
                                "p (qb c) -> p qb c", qb=NSUB)[:, :, 0:64]
                            dst = o162[par][:, :].rearrange(
                                "p (qb c) -> p qb c", qb=NSUB)[:, :, h * 64:(h + 1) * 64]
                            rz = zsc[:, hp * NSUB:(hp + 1) * NSUB].unsqueeze(2).broadcast_to(
                                [128, NSUB, 64])
                            return e.tensor_tensor(dst, srco, rz, ALU.mult)
                        dep_oz3 = [tk_oz2]
                        if h < 2 and ST["war_o16"][par] is not None:
                            dep_oz3.append(ST["war_o16"][par])
                        tk_oz3 = P.add("v", dep_oz3, t_oz3)
                        ST["o_evac"].append(tk_oz3)

                if weave is not None:
                    for _ in range(4):
                        next(weave, None)
            if sc < NSC - 1:
                def t_snap_all(e, snap_cur=snap_cur):
                    e.tensor_scalar(snap_cur[0][:, 0:H * 65], Sa[:, 0:H * 65],
                                    phcoc[:, 0:1], None, ALU.mult)
                    return e.tensor_scalar(snap_cur[1][:, 0:H * 65], Sb[:, 0:H * 65],
                                           phcoc[0:24, 1:2], None, ALU.mult)
                dep_snap = [tk_heads_done[-1][0]]
                if ST["snap_readers"][par] is not None:
                    dep_snap.append(ST["snap_readers"][par])
                tk_snapall = P.add("v", dep_snap, t_snap_all)
                ST["snap_written"][par] = tk_snapall
            F["tk_attn"] = ST["o_evac"][-1]

        def tail(sc):
            par = sc % 2
            F = FR[sc]
            tk_oT = None
            for grp in range(NSUB * 2):
                bi = grp % 2

                def t_ot(e, grp=grp, par=par):
                    bank = pstT if grp % 2 == 0 else pstT2
                    for i in range(4):
                        idx = grp * 4 + i
                        qb, hd = idx % NSUB, idx // NSUB
                        e.transpose(bank[:, i * 128:(i + 1) * 128],
                                    o162[par][:, qb * 1024 + hd * 128:qb * 1024 + (hd + 1) * 128],
                                    id16[:, :])
                    return None
                deps_ot = [F["tk_attn"]]
                if ST["last_pte"][bi] is not None:
                    deps_ot.append(ST["last_pte"][bi])
                tk_ott = P.add("t", deps_ot, t_ot)
                if grp == NSUB * 2 - 1:
                    ST["war_o16"][par] = tk_ott

                def t_ote(e, grp=grp):
                    bank = pstT if grp % 2 == 0 else pstT2
                    for i in range(4):
                        idx = grp * 4 + i
                        qb, hd = idx % NSUB, idx // NSUB
                        e.tensor_copy(oT[:, hd * SC + qb * 128:hd * SC + (qb + 1) * 128],
                                      bank[:, i * 128:(i + 1) * 128])
                    return None
                dep_ote = [tk_ott]
                if tk_oT is not None:
                    dep_ote.append(tk_oT)
                if ST["war_oT"] is not None and grp == 0:
                    dep_ote.append(ST["war_oT"])
                tk_oT = P.add("v", dep_ote, t_ote)
                ST["last_pte"][bi] = tk_oT
                yield

            X = xs2[par]
            tk_res_toks = []
            last_wo = None
            for ftile in range(NFT):
                def t_wo(e, ftile=ftile):
                    reg = psB[ftile % 2]
                    for kt in range(8):
                        e.matmul(reg[:, 0:SC],
                                 wo16[:, kt * 1024 + ftile * 128:kt * 1024 + (ftile + 1) * 128],
                                 oT[:, kt * SC:kt * SC + SC],
                                 start=(kt == 0), stop=(kt == 7))
                    return None
                dep_wo = depB(ftile % 2, tk_oT)
                if sc == 0 and ftile < 2:
                    dep_wo.append(tk_wof)
                tk_womm = P.add("t", dep_wo, t_wo)
                last_wo = tk_womm

                def t_res(e, ftile=ftile, X=X):
                    reg = psB[ftile % 2]
                    return e.tensor_tensor(X[:, ftile * SC:(ftile + 1) * SC],
                                           X[:, ftile * SC:(ftile + 1) * SC],
                                           reg[:, 0:SC], ALU.add)
                tk_res = P.add("v", [tk_womm], t_res)
                ST["war_B"][ftile % 2] = tk_res
                tk_res_toks.append(tk_res)
                yield
            ST["war_oT"] = last_wo
            tk_x2 = tk_res_toks[-1]

            def t_x2c(e, X=X, par=par):
                return e.tensor_copy(x162[par][:, :], X[:, :])
            tk_x2c = P.add("v", [tk_x2] + tk_res_toks, t_x2c)

            def t_sx2(e, par=par):
                for ft in range(NFT):
                    e.matmul(psB[0][0:1, 0:SC], o16c[:, 0:1],
                             x162[par][:, ft * SC:(ft + 1) * SC],
                             start=(ft == 0), stop=(ft == NFT - 1))
                return None
            tk_sx2 = P.add("t", depB(0, tk_x2c), t_sx2)
            ST["x16_war"][par] = tk_sx2
            yield

            def t_x2sq(e, X=X, par=par):
                return e.tensor_tensor(xsq2[par][:, :], X[:, :], X[:, :], ALU.mult)
            dep_x2sq = [tk_x2]
            if ST["war_xsq"] is not None:
                dep_x2sq.append(ST["war_xsq"])
            tk_x2sq = P.add("v", dep_x2sq, t_x2sq)

            def t_sxx2(e, par=par):
                for ft in range(NFT):
                    e.matmul(psB[1][0:1, 0:SC], o16c[:, 0:1],
                             xsq2[par][:, ft * SC:(ft + 1) * SC],
                             start=(ft == 0), stop=(ft == NFT - 1))
                return None
            tk_sxx2 = P.add("t", depB(1, tk_x2sq), t_sxx2)
            ST["war_xsq"] = tk_sxx2
            yield

            def t_l2v(e, par=par):
                sr = srow2[par]
                P.chain(e, "v", e.tensor_scalar(sr[0:1, 0:SC], psB[0][0:1, 0:SC],
                                                -1.0 / D, None, ALU.mult))
                P.chain(e, "v", e.tensor_tensor(sr[0:1, 2 * SC:3 * SC],
                                                sr[0:1, 0:SC], sr[0:1, 0:SC],
                                                ALU.mult))
                e.scalar_tensor_tensor(sr[0:1, SC:2 * SC], psB[1][0:1, 0:SC],
                                       1.0 / D, sr[0:1, 2 * SC:3 * SC],
                                       ALU.mult, ALU.subtract)
                return e.tensor_copy(nm2x[0:1, :], sr[0:1, 0:SC])
            dep_l2v = [tk_sx2, tk_sxx2]
            if ST["war_nm2"] is not None:
                dep_l2v.append(ST["war_nm2"])
            tk_l2v = P.add("v", dep_l2v, t_l2v)
            ST["war_B"][0] = tk_l2v
            ST["war_B"][1] = tk_l2v

            def t_l2s(e, par=par):
                sr = srow2[par]
                P.chain(e, "s", e.activation(sr[0:1, 2 * SC:3 * SC],
                                             sr[0:1, SC:2 * SC], AF.Ln, bias=LN_EPS))
                e.activation(sr[0:1, 2 * SC:3 * SC], sr[0:1, 2 * SC:3 * SC],
                             AF.Exp, scale=-0.5)
                return None
            tk_l2s = P.add("s", [tk_l2v], t_l2s)

            def t_l2f(e, par=par):
                return e.tensor_copy(rs2x[0:1, :], srow2[par][0:1, 2 * SC:3 * SC])
            tk_l2f = P.add("v", [tk_l2s], t_l2f)
            yield

            def t_l2bc(e):
                e.matmul(psB[0][:, 0:SC], e0[:, 0:128], nm2x[:, 0:SC],
                         start=True, stop=True)
                e.matmul(psB[1][:, 0:SC], e0[:, 0:128], rs2x[:, 0:SC],
                         start=True, stop=True)
                return None
            tk_l2bc = P.add("t", depB(0, tk_l2f) + depB(1), t_l2bc)
            ST["war_nm2"] = tk_l2bc

            tk_l2bce = tk_l2bc
            yield

            tk_q_last = None
            for fp in range(NFT // 4):
                def t_q8(e, fp=fp, X=X, par=par):
                    a0 = fp * 4 * SC
                    t2 = t1q[:, 0:4 * SC].rearrange("p (f t) -> p f t", f=4)
                    xv2 = X[:, a0:a0 + 4 * SC].rearrange("p (f t) -> p f t", f=4)
                    nmb = psB[0][:, 0:SC].unsqueeze(1).broadcast_to([128, 4, SC])
                    rsb = psB[1][:, 0:SC].unsqueeze(1).broadcast_to([128, 4, SC])
                    P.chain(e, "v", e.tensor_tensor(t2, xv2, nmb, ALU.add))
                    P.chain(e, "v", e.scalar_tensor_tensor(t2, t2, sinv[:, 0:1],
                                                           rsb, ALU.mult, ALU.mult))
                    P.chain(e, "v", e.tensor_scalar(q8f[:, 0:4 * SC], t1q[:, 0:4 * SC],
                                                    240.0, -240.0, ALU.min, ALU.max))
                    return e.tensor_copy(xq16[:, a0:a0 + 4 * SC], q8f[:, 0:4 * SC])
                dep_q8 = [tk_l2bce]
                if fp == 0 and ST.get("xq_war") is not None:
                    dep_q8.append(ST["xq_war"])
                tk_q_last = P.add("v", dep_q8, t_q8)
                ST["war_B"][0] = tk_q_last
                ST["war_B"][1] = tk_q_last
                yield
                yield
            store_toks = []
            for ftile in range(NFT):
                def t_ffn(e, ftile=ftile, par=par):
                    reg = psB[ftile % 2]
                    for kt in range(8):
                        e.matmul(reg[:, 0:SC],
                                 wf16[:, kt * 1024 + ftile * 128:kt * 1024 + (ftile + 1) * 128],
                                 xq16[:, kt * SC:kt * SC + SC],
                                 start=(kt == 0), stop=(kt == 7))
                    return None
                dep_ffn = depB(ftile % 2, tk_q_last)
                if sc == 0 and ftile < 2:
                    dep_ffn.append(tk_wof)
                tk_fmm = P.add("t", dep_ffn, t_ffn)
                if ftile == NFT - 1:
                    ST["xq_war"] = tk_fmm

                def t_yep(e, ftile=ftile, X=X):
                    reg = psB[ftile % 2]
                    yb = ybuf[:, (ftile % 2) * SC:(ftile % 2 + 1) * SC]
                    P.chain(e, "v", e.scalar_tensor_tensor(
                        yb, reg[:, 0:SC], ssc[:, 0:1],
                        X[:, ftile * SC:(ftile + 1) * SC], ALU.mult, ALU.add))
                    return e.tensor_scalar(yb, yb, biasc[:, ftile:ftile + 1],
                                           None, ALU.add)
                dep_yep = [tk_fmm]
                if ST["ybuf_war"][ftile % 2] is not None:
                    dep_yep.append(ST["ybuf_war"][ftile % 2])
                tk_yep = P.add("v", dep_yep, t_yep)
                ST["war_B"][ftile % 2] = tk_yep

                semn = "st" if ftile % 2 == 0 else "st1"

                def t_store(e, ftile=ftile, sc=sc, semn=semn):
                    e.dma_start(
                        out=yv[:, ftile, sc * SC:(sc + 1) * SC],
                        in_=ybuf[:, (ftile % 2) * SC:(ftile % 2 + 1) * SC],
                    ).then_inc(P.sems[semn], 16)
                tk_st = P.dma("q", [tk_yep], t_store, semn, 1)
                store_toks.append(tk_st)
                ST["ybuf_war"][ftile % 2] = tk_st
                yield
            ST["last_x_consumer"][par] = store_toks[-1]

        def chain_gens(gs):
            for g in gs:
                for _ in g:
                    yield

        g0 = front14(0)
        for _ in g0:
            pass
        front5(0)
        tail_prev = None
        for sc in range(NSC):
            gens = []
            if tail_prev is not None:
                gens.append(tail_prev)
            if sc + 1 < NSC:
                gens.append(front14(sc + 1))
            weave = chain_gens(gens) if gens else None
            attention(sc, weave=weave)
            if weave is not None:
                for _ in weave:
                    pass
            tail_prev = tail(sc)
            for _ in range(4):
                next(tail_prev, None)
            if sc + 1 < NSC:
                front5(sc + 1)
        for _ in tail_prev:
            pass

        def t_final(e):
            e.wait_ge(P.sems["st"], 16 * P.dma_counts["st"])
            e.wait_ge(P.sems["st1"], 16 * P.dma_counts["st1"])
        P.tasks["q"].append(([], lambda e: (t_final(e), None)[1], ("st", 0)))

        block = ent(nc.Block())
        P.emit(block)
    return nc


def _prep_core_inputs(b, x, Wq, Wk, Wv, Wo, gamma, beta, w8, bias, s_in, s_w):
    d = {}
    d["xT"] = np.ascontiguousarray(x[b].T.astype(np.float32))
    d["wq"] = np.ascontiguousarray(Wq.T)
    d["wk"] = np.ascontiguousarray(Wk.T)
    d["wv"] = np.ascontiguousarray(Wv.T)
    d["wo"] = np.ascontiguousarray(Wo.T)
    d["wf"] = np.ascontiguousarray(w8.T)
    d["g_col"] = np.ascontiguousarray(np.tile(gamma, 8)[:, None].astype(np.float32))
    d["b_col"] = np.ascontiguousarray(np.tile(beta, 8)[:, None].astype(np.float32))
    d["bias_c"] = np.ascontiguousarray(bias.reshape(8, 128).T.astype(np.float32))
    d["s_io"] = np.array([[s_in.reshape(-1)[0], s_w.reshape(-1)[0]]], np.float32)
    _co = [2.0] * 136 + [-1.0] * 16
    _pl = np.zeros((128, 128), np.float16)
    for _hl in range(8):
        for _fd in range(16):
            _pl[_hl * 16 + _fd, _fd * 8 + _hl] = 1
    d["pil"] = _pl
    _pc = np.zeros((128, 2), np.float32)
    _pc[:, 0] = _co[0:128]
    _pc[0:24, 1] = _co[128:152]
    d["phco"] = _pc
    _tr = np.triu(np.ones((128, 128), np.float16))
    d["tri"] = np.concatenate(
        [_tr, np.ones((128, 128), np.float16), _tr], axis=1)
    d["id16"] = np.eye(128, dtype=np.float16)
    ind8 = np.zeros((8, 128), np.float16)
    for g in range(8):
        ind8[g, g * 16:(g + 1) * 16] = 1
    d["ind8"] = ind8
    onesg = np.zeros((128, 8), np.float16)
    for g in range(8):
        onesg[g * 16:(g + 1) * 16, g] = 1
    d["onesg"] = onesg
    e0 = np.zeros((128, 128), np.float16)
    e0[0, :] = 1
    d["e0"] = e0
    d["o16c"] = np.ones((128, 1), np.float16)
    d["csth"] = np.broadcast_to(
        np.array([LN_EPS, 0.25, 1.0, 0.0, -0.5, -1.0, 240.0, 16.0 * LN_EPS,
                  16.0 * LN_EPS * LN_EPS], np.float32), (128, 9)).copy()
    return d


def kernel(L0_attn_x, L0_attn_Wq, L0_attn_Wk, L0_attn_Wv, L0_attn_Wo,
           L0_attn_gamma, L0_attn_beta, L0_ffn_weight_fp8, L0_ffn_bias,
           L0_ffn_input_scale, L0_ffn_weight_scale):
    args = (np.asarray(L0_attn_x, np.float32), np.asarray(L0_attn_Wq, np.float32),
            np.asarray(L0_attn_Wk, np.float32), np.asarray(L0_attn_Wv, np.float32),
            np.asarray(L0_attn_Wo, np.float32), np.asarray(L0_attn_gamma, np.float32),
            np.asarray(L0_attn_beta, np.float32), np.asarray(L0_ffn_weight_fp8, np.float32),
            np.asarray(L0_ffn_bias, np.float32), np.asarray(L0_ffn_input_scale, np.float32),
            np.asarray(L0_ffn_weight_scale, np.float32))
    x = args[0]
    B, L, _ = x.shape
    nc = build(L=L)
    in_maps = [_prep_core_inputs(b, *args) for b in range(B)]
    res = run_bass_kernel_spmd(nc, in_maps, core_ids=list(range(B)))
    out = np.empty((B, L, D), np.float32)
    for b in range(B):
        out[b] = res.results[b]["yT"].T
    return out


# version 40 (final)
# speedup vs baseline: 1.0613x; 1.0410x over previous
import contextlib
import numpy as np

import concourse.bass as bass
import concourse.mybir as mybir
from concourse.bass_utils import run_bass_kernel_spmd

F32 = mybir.dt.float32
F16 = mybir.dt.float16
F8 = mybir.dt.float8e4
AF = mybir.ActivationFunctionType
ALU = mybir.AluOpType

D = 1024
H = 16
FD = 16
HD = 64
NFT = D // 128
PHI = 152
EPS = 1e-5
LN_EPS = 1e-5
SUB = 128
MMN = 256

PAIR_OFF = [0]
for _a in range(16):
    PAIR_OFF.append(PAIR_OFF[-1] + (16 - _a))


def _ceil(a, b):
    return (a + b - 1) // b


class Prog:

    ENGS = ("t", "v", "s", "q", "g")

    def __init__(self, nc):
        self.nc = nc
        self.tasks = {e: [] for e in self.ENGS}
        self.sems = {}
        self.counts = {e: 0 for e in self.ENGS}
        self.dma_counts = {"qd": 0, "gd": 0, "gd2": 0, "st": 0, "st1": 0,
                           "rpq": 0, "rpk": 0}

    def make_sems(self, ent):
        for e in self.ENGS:
            self.sems[e] = ent(self.nc.semaphore(f"sem_{e}"))
            self.sems["ss_" + e] = ent(self.nc.semaphore(f"ssem_{e}"))
        for d in ("qd", "gd", "gd2", "st", "st1", "rpq", "rpk"):
            self.sems[d] = ent(self.nc.semaphore(f"sem_{d}"))
        self.self_cnt = {e: 0 for e in self.ENGS}

    def chain(self, e, eng, instr):
        self.self_cnt[eng] += 1
        instr.then_inc(self.sems["ss_" + eng], 1)
        e.wait_ge(self.sems["ss_" + eng], self.self_cnt[eng])

    def add(self, eng, deps, fn):
        self.tasks[eng].append((list(deps), fn, None))
        self.counts[eng] += 1
        return (self.sems[eng], self.counts[eng])

    def dma(self, eng, deps, fn, sem_name, n=1):
        self.dma_counts[sem_name] += n
        self.tasks[eng].append((list(deps), fn, (sem_name, n)))
        return (self.sems[sem_name], 16 * self.dma_counts[sem_name])

    def emit(self, block):
        nc = self.nc

        class _Rec:
            def __init__(self, eng):
                self._e = eng
                self.last = None

            def __getattr__(self, n):
                a = getattr(self._e, n)
                if callable(a):
                    def w(*args, **kw):
                        r = a(*args, **kw)
                        if hasattr(r, "then_inc"):
                            self.last = r
                        return r
                    return w
                return a

        def mk(e):
            def body(eng):
                waited = {}
                rec = _Rec(eng)
                n_done = 0
                for deps, fn, dmainfo in self.tasks[e]:
                    if n_done > 0 and e in ("v", "s"):
                        eng.wait_ge(self.sems[e], n_done)
                    n_done += 1
                    for (sem, val) in deps:
                        key = id(sem)
                        if waited.get(key, -1) < val:
                            eng.wait_ge(sem, val)
                            waited[key] = val
                    rec.last = None
                    ret = fn(rec)
                    if dmainfo is None:
                        last = ret if ret is not None else rec.last
                        if last is None:
                            last = eng.nop()
                        last.then_inc(self.sems[e], 1)
            return body

        for e, attr in (("t", "tensor"), ("v", "vector"), ("s", "scalar"),
                        ("q", "sync"), ("g", "gpsimd")):
            if self.tasks[e]:
                getattr(block, attr)(mk(e))


def build(L=4096, SC=256, acc_group=16):
    NSC = L // SC
    NSUB = SC // SUB
    NNH = _ceil(SC, MMN)
    nc = bass.Bass("TRN2", target_bir_lowering=False, debug=False)

    xT = nc.dram_tensor("xT", [D, L], F32, kind="ExternalInput").ap()
    wq = nc.dram_tensor("wq", [D, 256], F32, kind="ExternalInput").ap()
    wk = nc.dram_tensor("wk", [D, 256], F32, kind="ExternalInput").ap()
    wv = nc.dram_tensor("wv", [D, D], F32, kind="ExternalInput").ap()
    wo = nc.dram_tensor("wo", [D, D], F32, kind="ExternalInput").ap()
    wf = nc.dram_tensor("wf", [D, D], F32, kind="ExternalInput").ap()
    g_col = nc.dram_tensor("g_col", [128, 1], F32, kind="ExternalInput").ap()
    b_col = nc.dram_tensor("b_col", [128, 1], F32, kind="ExternalInput").ap()
    bias_c = nc.dram_tensor("bias_c", [128, 8], F32, kind="ExternalInput").ap()
    s_io = nc.dram_tensor("s_io", [1, 2], F32, kind="ExternalInput").ap()
    tri_h = nc.dram_tensor("tri", [128, 384], F16, kind="ExternalInput").ap()
    phco_h = nc.dram_tensor("phco", [128, 2], F32, kind="ExternalInput").ap()
    pil_h = nc.dram_tensor("pil", [128, 128], F16, kind="ExternalInput").ap()
    id16_h = nc.dram_tensor("id16", [128, 128], F16, kind="ExternalInput").ap()
    ind8_h = nc.dram_tensor("ind8", [8, 128], F16, kind="ExternalInput").ap()
    onesg_h = nc.dram_tensor("onesg", [128, 8], F16, kind="ExternalInput").ap()
    e0_h = nc.dram_tensor("e0", [128, 128], F16, kind="ExternalInput").ap()
    o16c_h = nc.dram_tensor("o16c", [128, 1], F16, kind="ExternalInput").ap()
    cst_h = nc.dram_tensor("csth", [128, 9], F32, kind="ExternalInput").ap()
    yT = nc.dram_tensor("yT", [D, L], F32, kind="ExternalOutput").ap()

    with contextlib.ExitStack() as stack:
        ent = stack.enter_context
        sb = lambda name, shape, dt=F32: ent(nc.sbuf_tensor(name, shape, dt))
        psmaker = lambda name, shape, dt=F32: ent(nc.psum_tensor(name, shape, dt))

        xs2 = [sb(f"xs{i}", [128, NFT * SC]) for i in range(2)]
        x162 = [sb(f"x16_{i}", [128, NFT * SC], F16) for i in range(2)]
        wq16 = sb("wq16", [128, 8 * 256], F16)
        wk16 = sb("wk16", [128, 8 * 256], F16)
        wv16 = sb("wv16", [128, 8 * 1024], F16)
        wo16 = sb("wo16", [128, 8 * 1024], F16)
        wf16 = sb("wf16", [128, 8 * 1024], F16)
        wq1x = sb("wq1x", [128, 256], F16)
        wk1x = sb("wk1x", [128, 256], F16)
        wv1x = sb("wv1x", [128, 1024], F16)
        negmux2 = [sb(f"negmux_{i}", [128, SC], F16) for i in range(2)]
        epscx2 = [sb(f"epscx_{i}", [128, SC], F16) for i in range(2)]
        nm2x = sb("nm2x", [128, SC], F16)
        rs2x = sb("rs2x", [128, SC], F16)
        srow2 = [sb(f"srow_{i}", [1, 3 * SC]) for i in range(2)]
        rc2 = [sb(f"rc_{i}", [128, 4]) for i in range(2)]
        fmu2 = [sb(f"fmu_{i}", [8, 4 * SC], F16) for i in range(2)]
        fvarS = sb("fvar_s", [8, 4 * SC], F16)
        flnxS = sb("flnx_s", [8, 4 * SC])
        fR2 = [sb(f"fR_{i}", [8, 4 * SC], F16) for i in range(2)]
        q162 = [sb(f"q16_{i}", [128, 2 * SC], F16) for i in range(2)]
        k162 = [sb(f"k16_{i}", [128, 2 * SC], F16) for i in range(2)]
        qh2 = [sb(f"qh_{i}", [128, 2 * SC], F16) for i in range(2)]
        kh2 = [sb(f"kh_{i}", [128, 2 * SC], F16) for i in range(2)]
        qhp2 = [sb(f"qhp{i}", [128, 6 * SC], F16) for i in range(2)]
        khp2 = [sb(f"khp{i}", [128, 6 * SC], F16) for i in range(2)]
        qil = sb("qil", [128, NSUB * 256], F16)
        kil = sb("kil", [128, NSUB * 256], F16)
        phq = sb("phq", [128, NSUB * PHI * 16], F16)
        phk = sb("phk", [128, NSUB * PHI * 16], F16)
        phqT = sb("phqT", [128, 2 * 512], F16)
        vt2 = [sb(f"vt_{i}", [128, NSUB * H * 65], F16) for i in range(2)]
        SSW = sum(SC - kb * 128 for kb in range(SC // SUB))
        ssq = sb("ssq", [128, 2 * SSW], F16)
        Sa = sb("Sa", [128, H * 65])
        Sb = sb("Sb", [24, H * 65])
        snapa = [sb(f"snapa{i}", [128, H * 65], F16) for i in range(2)]
        snapb = [sb(f"snapb{i}", [24, H * 65], F16) for i in range(2)]
        o162S = sb("o16_s", [128, NSUB * 1024], F16)
        o162 = [o162S, o162S]
        xq16 = sb("xq16", [128, NFT * SC], F16)
        oT = sb("oT", [128, 8 * SC], F16)
        t1f = sb("t1f", [128, 2 * SC], F16)
        t1q = sb("t1q", [128, 4 * SC], F16)

        q8f = sb("q8f", [128, 4 * SC], F8)
        ybuf = sb("ybuf", [128, 2 * SC])
        xsq2 = [sb(f"xsq_{i}", [128, NFT * SC], F16) for i in range(2)]
        sq32 = sb("sq32", [128, 2 * SSW])
        tri = sb("tri_s", [128, 384], F16)
        phcoc = sb("phcoc", [128, 2], F32)
        pil = sb("pil_s", [128, 128], F16)
        id16 = sb("id16_s", [128, 128], F16)
        ind8 = sb("ind8_s", [8, 128], F16)
        onesg = sb("onesg_s", [128, 8], F16)
        e0 = sb("e0_s", [128, 128], F16)
        o16c = sb("o16c_s", [128, 1], F16)
        cstc = sb("cstc", [128, 9])
        gq = sb("gq", [128, 1])
        bq = sb("bq", [128, 1])
        gk = sb("gk", [128, 1])
        bk = sb("bk", [128, 1])
        biasc = sb("biasc", [128, 8])
        sioc = sb("sioc", [1, 2])
        sinv = sb("sinv", [128, 1])
        ssc = sb("ssc", [128, 1])
        zsc = sb("zsc", [128, 8])

        (C_LNEPS, C_QSC, C_ONE, C_Z, C_HALFNEG, C_NEG1, C_CLIP, C_EPS16,
         C_EPSSQ) = range(9)
        nc.const_aps.register(F32, LN_EPS, cstc[:, C_LNEPS:C_LNEPS + 1])
        nc.const_aps.register(F32, 0.25, cstc[:, C_QSC:C_QSC + 1])
        nc.const_aps.register(F32, 1.0, cstc[:, C_ONE:C_ONE + 1])
        nc.const_aps.register(F32, 0.0, cstc[:, C_Z:C_Z + 1])
        nc.const_aps.register(F32, -0.5, cstc[:, C_HALFNEG:C_HALFNEG + 1])
        nc.const_aps.register(F32, -1.0, cstc[:, C_NEG1:C_NEG1 + 1])
        nc.const_aps.register(F32, 240.0, cstc[:, C_CLIP:C_CLIP + 1])
        nc.const_aps.register(F32, 16.0 * LN_EPS, cstc[:, C_EPS16:C_EPS16 + 1])
        nc.const_aps.register(F32, 16.0 * LN_EPS * LN_EPS,
                              cstc[:, C_EPSSQ:C_EPSSQ + 1])

        ps = [psmaker(f"ps{i}", [128, 512]) for i in range(5)]
        pstT = psmaker("pstT", [128, 1024], F16)
        pstT2 = psmaker("pstT2", [128, 1024], F16)
        ps7 = psmaker("ps7", [128, 512])

        P = Prog(nc)
        P.make_sems(ent)

        def t_init_consts(e):
            e.dma_start(out=tri[:, :], in_=tri_h).then_inc(P.sems["qd"], 16)
            e.dma_start(out=id16[:, :], in_=id16_h).then_inc(P.sems["qd"], 16)
            e.dma_start(out=ind8[:, :], in_=ind8_h).then_inc(P.sems["qd"], 16)
            e.dma_start(out=onesg[:, :], in_=onesg_h).then_inc(P.sems["qd"], 16)
            e.dma_start(out=e0[:, :], in_=e0_h).then_inc(P.sems["qd"], 16)
            e.dma_start(out=o16c[:, :], in_=o16c_h).then_inc(P.sems["qd"], 16)
            e.dma_start(out=cstc[:, :], in_=cst_h).then_inc(P.sems["qd"], 16)
            e.dma_start(out=gq[:, :], in_=g_col).then_inc(P.sems["qd"], 16)
            e.dma_start(out=gk[:, :], in_=g_col).then_inc(P.sems["qd"], 16)
            e.dma_start(out=bq[:, :], in_=b_col).then_inc(P.sems["qd"], 16)
            e.dma_start(out=bk[:, :], in_=b_col).then_inc(P.sems["qd"], 16)
            e.dma_start(out=biasc[:, :], in_=bias_c).then_inc(P.sems["qd"], 16)
            e.dma_start(out=sioc[:, :], in_=s_io).then_inc(P.sems["qd"], 16)
            e.dma_start(out=phcoc[:, :], in_=phco_h).then_inc(P.sems["qd"], 16)
            e.dma_start(out=pil[:, :], in_=pil_h).then_inc(P.sems["qd"], 16)
        tk_consts = P.dma("q", [], t_init_consts, "qd", 15)

        def t_init_wqkv(e):
            for dst, srct, nf in ((wq16, wq, 256), (wk16, wk, 256),
                                  (wv16, wv, 1024)):
                e.dma_start(
                    out=dst[:, :].rearrange("p (kt f) -> p kt f", f=nf),
                    in_=srct.rearrange("(kt p) f -> p kt f", p=128),
                ).then_inc(P.sems["gd"], 16)
        tk_w = P.dma("g", [], t_init_wqkv, "gd", 3)

        def t_init_wof(e):
            for dst, srct, nf in ((wo16, wo, 1024), (wf16, wf, 1024)):
                e.dma_start(
                    out=dst[:, :].rearrange("p (kt f) -> p kt f", f=nf),
                    in_=srct.rearrange("(kt p) f -> p kt f", p=128),
                ).then_inc(P.sems["gd2"], 16)
        tk_wof = P.dma("g", [], t_init_wof, "gd2", 2)

        def t_init_zero(e):
            for i in range(2):
                e.memset(negmux2[i][:, :], 0.0)
                e.memset(epscx2[i][:, :], 0.0)
            e.memset(nm2x[:, :], 0.0)
            e.memset(rs2x[:, :], 0.0)
            e.memset(wq1x[:, :], 0.0)
            e.memset(wk1x[:, :], 0.0)
            e.memset(wv1x[:, :], 0.0)
            e.memset(Sa[:, :], 0.0)
            e.memset(Sb[:, :], 0.0)
            for i in range(2):
                e.memset(qhp2[i][:, :], 0.0)
                e.memset(khp2[i][:, :], 0.0)
            for i in range(2):
                ap = vt2[i][:, :].rearrange("p (n h c) -> p n h c",
                                            n=NSUB, h=H)[:, :, :, 64:65]
                e.memset(ap, 1.0)
            return None
        tk_zero = P.add("v", [], t_init_zero)

        def t_init_scale(e):
            e.activation(gq[:, :], gq[:, :], AF.Identity, scale=0.25)
            e.activation(bq[:, :], bq[:, :], AF.Identity, scale=0.25)
            return None
        tk_scale = P.add("s", [tk_consts], t_init_scale)

        def t_init_sio_rows(e):
            P.chain(e, "s", e.activation(srow2[0][:, 0:1], sioc[:, 0:1], AF.Ln))
            e.activation(srow2[0][:, 0:1], srow2[0][:, 0:1], AF.Exp,
                         scale=cstc[0:1, C_NEG1:C_NEG1 + 1])
            return None
        tk_sio1 = P.add("s", [tk_consts, tk_scale], t_init_sio_rows)

        def t_init_sio_mul(e):
            P.chain(e, "v", e.tensor_tensor(srow2[0][:, 1:2], sioc[:, 0:1],
                                            sioc[:, 1:2], ALU.mult))
            return e.tensor_copy(negmux2[0][0:1, 0:2], srow2[0][0:1, 0:2])
        tk_sio2 = P.add("v", [tk_sio1, tk_zero], t_init_sio_mul)

        def t_init_sio_bcast(e):
            e.matmul(ps7[:, 0:2], e0[:, 0:128], negmux2[0][:, 0:2],
                     start=True, stop=True)
            return None
        tk_sio3 = P.add("t", [tk_sio2, tk_consts], t_init_sio_bcast)

        def t_init_sio_evac(e):
            e.tensor_copy(sinv[:, :], ps7[:, 0:1])
            e.tensor_copy(ssc[:, :], ps7[:, 1:2])
            return e.memset(negmux2[0][0:1, 0:2], 0.0)
        tk_sio4 = P.add("v", [tk_sio3], t_init_sio_evac)

        def t_init_w1qk(e):
            for w16t, lo in ((wq16, 0), (wk16, 256)):
                for kt in range(8):
                    e.matmul(ps7[0:1, lo:lo + 256], o16c[:, 0:1],
                             w16t[:, kt * 256:kt * 256 + 256],
                             start=(kt == 0), stop=(kt == 7))
            return None
        tk_w1qk = P.add("t", [tk_w, tk_consts, tk_sio4], t_init_w1qk)

        def t_init_w1qk_evac(e):
            e.tensor_copy(wq1x[0:1, :], ps7[0:1, 0:256])
            return e.tensor_copy(wk1x[0:1, :], ps7[0:1, 256:512])
        tk_w1qke = P.add("v", [tk_w1qk, tk_zero, tk_sio4], t_init_w1qk_evac)

        tk_prev_v = tk_w1qke
        tk_prev_t = tk_w1qk
        for c in range(4):
            def t_wv1_mm(e, c=c, dep=None):
                for kt in range(8):
                    e.matmul(ps[0][0:1, 0:256], o16c[:, 0:1],
                             wv16[:, kt * 1024 + c * 256:kt * 1024 + (c + 1) * 256],
                             start=(kt == 0), stop=(kt == 7))
                return None
            tk_prev_t = P.add("t", [tk_w, tk_prev_v], t_wv1_mm)

            def t_wv1_ev(e, c=c, dep_t=tk_prev_t):
                return e.tensor_copy(wv1x[0:1, c * 256:(c + 1) * 256],
                                     ps[0][0:1, 0:256])
            tk_prev_v = P.add("v", [tk_prev_t], t_wv1_ev)
        tk_wv1 = tk_prev_v

        psB = [ps[0], ps[3]]
        ST = {
            "last_x_consumer": [None, None],
            "x16_war": [None, None],
            "snap_written": [None, None],
            "snap_readers": [None, None],
            "sq_sT_war": None,
            "omm_ssq_war": [None, None],
            "sqv_war": [None, None],
            "last_pte": [None, None],
            "last_omm_pt": [None, None],
            "pte_cur": None,
            "war_vt": [None, None],
            "war_phq": None,
            "war_phk": None,
            "war_qhp": [None, None],
            "war_B": [None, None],
            "war_negmu": [None, None],
            "war_epsc": [None, None],
            "war_nm2": None,
            "war_o16": [None, None],
            "war_oT": None,
            "war_xsq": None,
            "war_fvar": None,
            "xq_war": None,
            "ybuf_war": [None, None],
            "o_evac": [],
            "oz2_prev": None,
            "upd_prev": None,
        }

        def t_pstT_init(e):
            for j in range(4):
                e.transpose(pstT[:, 512 + j * 128:512 + (j + 1) * 128],
                            id16[:, :], id16[:, :])
                e.transpose(pstT2[:, 512 + j * 128:512 + (j + 1) * 128],
                            id16[:, :], id16[:, :])
            return None
        tk_pstT_init = P.add("t", [tk_consts], t_pstT_init)
        ST["last_pte"][0] = tk_pstT_init
        ST["last_pte"][1] = tk_pstT_init
        FR = [dict() for _ in range(NSC)]
        xv = xT.rearrange("(f p) L -> p f L", p=128)
        yv = yT.rearrange("(f p) L -> p f L", p=128)
        HMAP = [(0, [0, 1, 2]), (1, [3, 4, 5]), (2, [6, 7]),
                (3, [8, 9, 10]), (4, [11, 12, 13]), (5, [14, 15])]
        spans = [SC - kb * 128 for kb in range(NSUB)]
        sso = [0]
        for kb in range(1, NSUB):
            sso.append(sso[-1] + spans[kb - 1])

        def depB(i, *toks):
            d = [t for t in toks if t is not None]
            if ST["war_B"][i] is not None:
                d.append(ST["war_B"][i])
            return d

        def front14(sc):
            par = sc % 2
            X, X16 = xs2[par], x162[par]
            F = FR[sc]

            deps = [tk_consts]
            if ST["last_x_consumer"][par] is not None:
                deps.append(ST["last_x_consumer"][par])

            def t_xload(e, sc=sc, X=X):
                e.dma_start(
                    out=X[:, :].rearrange("p (f t) -> p f t", f=NFT),
                    in_=xv[:, :, sc * SC:(sc + 1) * SC],
                ).then_inc(P.sems["qd"], 16)
            tk_x = P.dma("q", deps, t_xload, "qd", 1)
            yield

            deps16 = [tk_x, tk_zero]
            if ST["x16_war"][par] is not None:
                deps16.append(ST["x16_war"][par])

            def t_x16(e, X=X, X16=X16):
                return e.tensor_copy(X16[:, :], X[:, :])
            tk_x16 = P.add("v", deps16, t_x16)

            def t_xsq(e, X=X, par=par):
                return e.tensor_tensor(xsq2[par][:, :], X[:, :], X[:, :], ALU.mult)
            dep_xsq = [tk_x]
            if ST["war_xsq"] is not None:
                dep_xsq.append(ST["war_xsq"])
            tk_xsq = P.add("v", dep_xsq, t_xsq)
            yield

            def t_sx(e, X16=X16):
                for ft in range(NFT):
                    e.matmul(psB[0][0:1, 0:SC], o16c[:, 0:1],
                             X16[:, ft * SC:(ft + 1) * SC],
                             start=(ft == 0), stop=(ft == NFT - 1))
                return None
            tk_sx = P.add("t", depB(0, tk_x16, tk_consts), t_sx)

            def t_sxx(e, par=par):
                for ft in range(NFT):
                    e.matmul(psB[1][0:1, 0:SC], o16c[:, 0:1],
                             xsq2[par][:, ft * SC:(ft + 1) * SC],
                             start=(ft == 0), stop=(ft == NFT - 1))
                return None
            tk_sxx = P.add("t", depB(1, tk_xsq), t_sxx)
            ST["war_xsq"] = tk_sxx
            yield

            def t_stats_v(e, par=par):
                sr = srow2[par]
                P.chain(e, "v", e.tensor_scalar(sr[0:1, 0:SC], psB[0][0:1, 0:SC],
                                                -1.0 / D, None, ALU.mult))
                P.chain(e, "v", e.tensor_tensor(sr[0:1, 2 * SC:3 * SC],
                                                sr[0:1, 0:SC], sr[0:1, 0:SC],
                                                ALU.mult))
                e.scalar_tensor_tensor(sr[0:1, SC:2 * SC], psB[1][0:1, 0:SC],
                                       1.0 / D, sr[0:1, 2 * SC:3 * SC],
                                       ALU.mult, ALU.subtract)
                return e.tensor_copy(negmux2[par][0:1, :], sr[0:1, 0:SC])
            dep_stv = [tk_sx, tk_sxx]
            if ST["war_negmu"][par] is not None:
                dep_stv.append(ST["war_negmu"][par])
            tk_stv = P.add("v", dep_stv, t_stats_v)
            ST["war_B"][0] = tk_stv
            ST["war_B"][1] = tk_stv

            def t_stats_s(e, par=par):
                sr = srow2[par]
                P.chain(e, "s", e.activation(sr[0:1, 2 * SC:3 * SC],
                                             sr[0:1, SC:2 * SC], AF.Ln, bias=LN_EPS))
                e.activation(sr[0:1, 2 * SC:3 * SC], sr[0:1, 2 * SC:3 * SC],
                             AF.Exp, scale=-0.5)
                return None
            tk_sts = P.add("s", [tk_stv], t_stats_s)

            def t_stats_v2(e, par=par):
                return e.tensor_scalar(epscx2[par][0:1, :], srow2[par][0:1, SC:2 * SC],
                                       LN_EPS, 16.0 * LN_EPS, ALU.add, ALU.mult)
            dep_stv2 = [tk_sts]
            if ST["war_epsc"][par] is not None:
                dep_stv2.append(ST["war_epsc"][par])
            tk_stv2 = P.add("v", dep_stv2, t_stats_v2)

            def t_rc_t(e, par=par):
                for j in range(NSUB):
                    e.transpose(psB[0][:, 320 + j:321 + j],
                                srow2[par][0:1, 2 * SC + j * 128:2 * SC + (j + 1) * 128],
                                cstc[0:1, C_ONE:C_ONE + 1])
                return None
            tk_rct = P.add("t", depB(0, tk_sts), t_rc_t)

            def t_rc_e(e, par=par):
                return e.copy(rc2[par][:, 0:NSUB], psB[0][:, 320:320 + NSUB])
            tk_rce = P.add("s", [tk_rct], t_rc_e)
            ST["war_B"][0] = tk_rce
            yield

            def mk_proj(w16t, w1t, dst16, par=par, X16=X16):
                def t_proj(e):
                    for m in range(2):
                        for kt in range(8):
                            e.matmul(psB[m][:, 0:SC],
                                     w16t[:, kt * 256 + m * 128:kt * 256 + (m + 1) * 128],
                                     X16[:, kt * SC:kt * SC + SC],
                                     start=(kt == 0), stop=False)
                        e.matmul(psB[m][:, 0:SC],
                                 w1t[:, m * 128:(m + 1) * 128],
                                 negmux2[par][:, 0:SC],
                                 start=False, stop=True)
                    return None

                def t_evac(e):
                    e.tensor_copy(dst16[:, 0:SC], psB[0][:, 0:SC])
                    return e.tensor_copy(dst16[:, SC:2 * SC], psB[1][:, 0:SC])
                return t_proj, t_evac

            t_pq, t_pqe = mk_proj(wq16, wq1x, q162[par])
            tk_pq = P.add("t", depB(0, tk_stv, tk_w1qke, tk_sio4) + depB(1),
                          t_pq)
            ST["war_negmu"][par] = tk_pq
            tk_pqe = P.add("v", [tk_pq], t_pqe)
            ST["war_B"][0] = tk_pqe
            ST["war_B"][1] = tk_pqe
            yield

            t_pk, t_pke = mk_proj(wk16, wk1x, k162[par])
            tk_pk = P.add("t", depB(0) + depB(1), t_pk)
            ST["war_negmu"][par] = tk_pk
            tk_pke = P.add("v", [tk_pk], t_pke)
            ST["war_B"][0] = tk_pke
            ST["war_B"][1] = tk_pke
            yield

            tk_ve = None
            for tsub in range(NSUB):
                def t_pv(e, tsub=tsub, par=par, X16=X16):
                    for c in range(4):
                        reg = psB[c % 2]
                        cl = (c // 2) * 256
                        for kt in range(8):
                            e.matmul(reg[:, cl:cl + 256],
                                     X16[:, kt * SC + tsub * 128:kt * SC + (tsub + 1) * 128],
                                     wv16[:, kt * 1024 + c * 256:kt * 1024 + (c + 1) * 256],
                                     start=(kt == 0), stop=False)
                        e.matmul(reg[:, cl:cl + 256],
                                 negmux2[par][:, tsub * 128:(tsub + 1) * 128],
                                 wv1x[:, c * 256:(c + 1) * 256],
                                 start=False, stop=True)
                    return None
                tk_pv = P.add("t", depB(0, tk_rce, tk_wv1) + depB(1), t_pv)
                ST["war_negmu"][par] = tk_pv

                def t_vev(e, tsub=tsub, par=par):
                    full = vt2[par][:, tsub * H * 65:(tsub + 1) * H * 65].rearrange(
                        "p (g h cc) -> p g h cc", g=4, h=4)
                    for bk in range(2):
                        d = full[:, bk::2, :, 0:64]
                        e.tensor_scalar(
                            d, psB[bk][:, 0:512].rearrange(
                                "p (g h cc) -> p g h cc", g=2, h=4)[:, :, :, 0:64],
                            rc2[par][:, tsub:tsub + 1], None, ALU.mult)
                    return None
                dep_ve = [tk_pv]
                if tsub == 0 and ST["war_vt"][par] is not None:
                    dep_ve.append(ST["war_vt"][par])
                tk_ve = P.add("v", dep_ve, t_vev)
                ST["war_B"][0] = tk_ve
                ST["war_B"][1] = tk_ve
                yield
            F["tk_v"] = tk_ve

            def t_fsq(e, par=par):
                e.tensor_tensor(xsq2[par][:, 0:2 * SC], q162[par][:, :],
                                q162[par][:, :], ALU.mult)
                return e.tensor_tensor(xsq2[par][:, 2 * SC:4 * SC], k162[par][:, :],
                                       k162[par][:, :], ALU.mult)
            tk_fsq = P.add("v", [tk_pqe, tk_pke, tk_sxx], t_fsq)

            def t_gs1(e, par=par):
                for side, srct in ((0, q162[par]), (1, k162[par])):
                    for ftile in range(2):
                        e.matmul(psB[side][0:8, ftile * SC:(ftile + 1) * SC],
                                 onesg[:, :], srct[:, ftile * SC:(ftile + 1) * SC],
                                 start=True, stop=True)
                return None
            tk_gs1 = P.add("t", depB(0, tk_pqe, tk_pke, tk_consts) + depB(1), t_gs1)

            def t_mu(e, par=par):
                e.tensor_scalar(fmu2[par][0:8, 0:2 * SC], psB[0][0:8, 0:2 * SC],
                                1.0 / FD, None, ALU.mult)
                return e.tensor_scalar(fmu2[par][0:8, 2 * SC:4 * SC],
                                       psB[1][0:8, 0:2 * SC],
                                       1.0 / FD, None, ALU.mult)
            tk_mu = P.add("v", [tk_gs1], t_mu)
            ST["war_B"][0] = tk_mu
            ST["war_B"][1] = tk_mu
            yield

            def t_gs2(e, par=par):
                for side in range(2):
                    for ftile in range(2):
                        reg = psB[side][0:8, ftile * SC:(ftile + 1) * SC]
                        e.matmul(reg, onesg[:, :],
                                 xsq2[par][:, side * 2 * SC + ftile * SC:
                                           side * 2 * SC + (ftile + 1) * SC],
                                 start=True, stop=False)
                        e.matmul(reg, e0[:, 0:8], epscx2[par][:, 0:SC],
                                 start=False, stop=True)
                return None
            tk_gs2 = P.add("t", depB(0, tk_fsq, tk_mu, tk_stv2) + depB(1), t_gs2)
            ST["war_epsc"][par] = tk_gs2

            def t_var(e, par=par):
                P.chain(e, "v", e.tensor_tensor(flnxS[0:8, :], fmu2[par][0:8, :],
                                                fmu2[par][0:8, :], ALU.mult))
                e.scalar_tensor_tensor(fvarS[0:8, 0:2 * SC],
                                       psB[0][0:8, 0:2 * SC],
                                       1.0 / FD, flnxS[0:8, 0:2 * SC],
                                       ALU.mult, ALU.subtract)
                return e.scalar_tensor_tensor(fvarS[0:8, 2 * SC:4 * SC],
                                              psB[1][0:8, 0:2 * SC],
                                              1.0 / FD, flnxS[0:8, 2 * SC:4 * SC],
                                              ALU.mult, ALU.subtract)
            dep_var = [tk_gs2]
            if ST["war_fvar"] is not None:
                dep_var.append(ST["war_fvar"])
            tk_var = P.add("v", dep_var, t_var)
            ST["war_B"][0] = tk_var
            ST["war_B"][1] = tk_var

            def t_R_s(e, par=par):
                P.chain(e, "s", e.activation(flnxS[0:8, :], fvarS[0:8, :],
                                             AF.Ln))
                e.activation(flnxS[0:8, :], flnxS[0:8, :], AF.Exp,
                             scale=-0.5)
                return None
            tk_Rs = P.add("s", [tk_var], t_R_s)
            ST["war_fvar"] = tk_Rs

            def t_R_v(e, par=par):
                return e.tensor_copy(fR2[par][0:8, :], flnxS[0:8, :])
            tk_Rv = P.add("v", [tk_Rs], t_R_v)
            yield

            for pair in range(4):
                side, ftile = pair // 2, pair % 2
                src16 = q162[par] if side == 0 else k162[par]
                dst = qh2[par] if side == 0 else kh2[par]
                gcol = gq if side == 0 else gk
                bcol = bq if side == 0 else bk
                bi = pair % 2

                def t_bc(e, side=side, ftile=ftile, bi=bi, par=par):
                    off = side * 2 * SC + ftile * SC
                    e.matmul(psB[bi][:, 0:SC], ind8[:, :], fmu2[par][0:8, off:off + SC],
                             start=True, stop=True)
                    e.matmul(psB[bi][:, SC:2 * SC], ind8[:, :], fR2[par][0:8, off:off + SC],
                             start=True, stop=True)
                    return None
                tk_bc = P.add("t", depB(bi, tk_Rv), t_bc)

                def t_norm(e, src16=src16, dst=dst, gcol=gcol, bcol=bcol,
                           ftile=ftile, bi=bi):
                    a0 = ftile * SC
                    tp1 = t1f[:, (ftile % 2) * SC:(ftile % 2 + 1) * SC]
                    mb = psB[bi][:, 0:SC]
                    Rb = psB[bi][:, SC:2 * SC]
                    P.chain(e, "v", e.tensor_tensor(tp1, src16[:, a0:a0 + SC],
                                                    mb, ALU.subtract))
                    P.chain(e, "v", e.scalar_tensor_tensor(tp1, tp1, gcol[:, 0:1],
                                                           Rb, ALU.mult, ALU.mult))
                    return e.tensor_scalar(dst[:, a0:a0 + SC], tp1,
                                           bcol[:, 0:1], None, ALU.add)
                tk_norm = P.add("v", [tk_bc], t_norm)
                ST["war_B"][bi] = tk_norm
                if pair == 1:
                    F["tk_qh"] = tk_norm
                if pair == 3:
                    F["tk_kh"] = tk_norm
                yield

        def front5(sc):
            par = sc % 2
            F = FR[sc]

            def mk_repack(srctn, dsttn, semname):
                def t_rp(e):
                    for d, hl in HMAP:
                        for s, hh in enumerate(hl):
                            ft = hh // 8
                            p0 = (hh % 8) * 16
                            e.dma_start(
                                out=dsttn[32 * s:32 * s + 16, d * SC:(d + 1) * SC],
                                in_=srctn[p0:p0 + 16, ft * SC:(ft + 1) * SC],
                            ).then_inc(P.sems[semname], 16)
                return t_rp
            dep_rp = [F["tk_qh"], tk_zero]
            if ST["war_qhp"][par] is not None:
                dep_rp.append(ST["war_qhp"][par])
            F["tk_qhp"] = P.dma("q", dep_rp, mk_repack(qh2[par], qhp2[par], "rpq"), "rpq", 16)
            dep_rk = [F["tk_kh"], tk_zero]
            if ST["war_qhp"][par] is not None:
                dep_rk.append(ST["war_qhp"][par])
            F["tk_khp"] = P.dma("g", dep_rk, mk_repack(kh2[par], khp2[par], "rpk"), "rpk", 16)

            def mk_il(srctn, dsttn, tk_src, bank, bi):
                def t_tr(e, srctn=srctn, bank=bank):
                    for i in range(2 * NSUB):
                        tsub, ftile = i % NSUB, i // NSUB
                        e.transpose(bank[:, tsub * 256 + ftile * 128:
                                         tsub * 256 + ftile * 128 + 128],
                                    srctn[:, ftile * SC + tsub * 128:
                                          ftile * SC + (tsub + 1) * 128],
                                    pil[:, :])
                    return None
                dep = [tk_src]
                if ST["last_pte"][bi] is not None:
                    dep.append(ST["last_pte"][bi])
                tk_tr = P.add("t", dep, t_tr)

                def t_ev(e, dsttn=dsttn, bank=bank):
                    return e.tensor_copy(dsttn[:, 0:2 * 256], bank[:, 0:2 * 256])
                tk_ev = P.add("v", [tk_tr], t_ev)
                ST["last_pte"][bi] = tk_ev
                return tk_ev
            tk_qil = mk_il(qh2[par], qil, F["tk_qh"], pstT, 0)
            tk_kil = mk_il(kh2[par], kil, F["tk_kh"], pstT2, 1)

            def mk_phi_task(iltn, dsttn, tsub, factor2, war):
                def t_phi(e, tsub=tsub):
                    src = iltn[:, tsub * 256:(tsub + 1) * 256]
                    srcv = src.rearrange("p (ft a hl) -> p a ft hl", ft=2, hl=8)
                    dst = dsttn[:, tsub * PHI * 16:(tsub + 1) * PHI * 16]
                    for a in range(16):
                        nb = 16 - a
                        o = PAIR_OFF[a] * 16
                        in0 = srcv[:, a:a + 1, :, :].broadcast_to([128, nb, 2, 8])
                        in1 = srcv[:, a:16, :, :]
                        outap = dst[:, o:o + nb * 16].rearrange(
                            "p (b ft hl) -> p b ft hl", ft=2, hl=8)
                        if factor2:
                            e.scalar_tensor_tensor(outap, in0, 2.0, in1,
                                                   ALU.mult, ALU.mult)
                        else:
                            e.tensor_tensor(outap, in0, in1, ALU.mult)
                    dd = dst[:, 136 * 16:152 * 16].rearrange(
                        "p (a ft hl) -> p a ft hl", ft=2, hl=8)
                    if factor2:
                        return e.scalar_tensor_tensor(dd, srcv, -1.0, srcv,
                                                      ALU.mult, ALU.mult)
                    return e.tensor_tensor(dd, srcv, srcv, ALU.mult)
                return t_phi
            for tsub in range(NSUB):
                dep = [tk_qil]
                if tsub == 0 and ST["war_phq"] is not None:
                    dep.append(ST["war_phq"])
                F[f"tk_phq{tsub}"] = P.add("v", dep, mk_phi_task(qil, phq, tsub, False, None))
            for tsub in range(NSUB):
                dep = [tk_kil]
                if tsub == 0 and ST["war_phk"] is not None:
                    dep.append(ST["war_phk"])
                F[f"tk_phk{tsub}"] = P.add("v", dep, mk_phi_task(kil, phk, tsub, False, None))
            F["tk_phk"] = F[f"tk_phk{NSUB - 1}"]

        def attention(sc, weave=None):
            par = sc % 2
            F = FR[sc]
            snap_cur = snapa[par], snapb[par]
            snap_prev = snapa[1 - par], snapb[1 - par]
            tk_heads_done = []
            last_sT = None
            pending_oz = [None]
            for h in range(H):
                hp = h % 2
                d_tile, hl = next((d, l) for d, l in HMAP if h in l)
                slot = hl.index(h)
                p0 = 32 * slot
                kq_t0 = d_tile * SC

                def t_sT(e, p0=p0, kq_t0=kq_t0, par=par):
                    for kb in range(NSUB):
                        span = spans[kb]
                        lhs = khp2[par][p0:p0 + 16, kq_t0 + kb * 128:kq_t0 + (kb + 1) * 128]
                        for nchunk in range(_ceil(span, MMN)):
                            c0 = nchunk * MMN
                            c1 = min(span, c0 + MMN)
                            e.matmul(ps[2][:, sso[kb] + c0:sso[kb] + c1], lhs,
                                     qhp2[par][p0:p0 + 16,
                                               kq_t0 + kb * 128 + c0:kq_t0 + kb * 128 + c1],
                                     start=True, stop=True)
                    return None
                dep_mm = [F["tk_qhp"], F["tk_khp"]]
                if ST["sq_sT_war"] is not None:
                    dep_mm.append(ST["sq_sT_war"])
                tk_mm = P.add("t", dep_mm, t_sT)
                last_sT = tk_mm

                def t_sq_s(e, hp=hp):
                    return e.activation(sq32[:, hp * SSW:(hp + 1) * SSW],
                                        ps[2][:, 0:SSW], AF.Square)
                dep_sqs = [tk_mm]
                if ST["sqv_war"][hp] is not None:
                    dep_sqs.append(ST["sqv_war"][hp])
                tk_sqs = P.add("s", dep_sqs, t_sq_s)
                ST["sq_sT_war"] = tk_sqs

                def t_sq_v(e, hp=hp):
                    return e.tensor_tensor(ssq[:, hp * SSW:(hp + 1) * SSW],
                                           sq32[:, hp * SSW:(hp + 1) * SSW],
                                           tri[:, 0:SSW], ALU.mult)
                dep_sq = [tk_sqs]
                if ST["omm_ssq_war"][hp] is not None:
                    dep_sq.append(ST["omm_ssq_war"][hp])
                tk_sq = P.add("v", dep_sq, t_sq_v)
                ST["sqv_war"][hp] = tk_sq
                if pending_oz[0] is not None:
                    pending_oz[0]()
                    pending_oz[0] = None

                if (h % 3 == 2 or h == H - 1) and sc < NSC - 1:
                    hh_rd = (h - 2, h - 1, h) if h % 3 == 2 else (h,)

                    def t_upd(e, hh_rd=hh_rd):
                        for j, hx in enumerate(hh_rd):
                            for tsub in range(NSUB):
                                lh = phk[:, tsub * PHI * 16:(tsub + 1) * PHI * 16].rearrange(
                                    "p (f hh) -> p f hh", hh=16)
                                lhsa = lh[:, 0:128, hx:hx + 1].squeeze(2)
                                rhs = vt2[par][:, tsub * H * 65 + hx * 65:
                                               tsub * H * 65 + (hx + 1) * 65]
                                e.matmul(ps7[:, j * 65:(j + 1) * 65], lhsa, rhs,
                                         start=(tsub == 0), stop=(tsub == NSUB - 1))
                        for j, hx in enumerate(hh_rd):
                            for tsub in range(NSUB):
                                lh = phk[:, tsub * PHI * 16:(tsub + 1) * PHI * 16].rearrange(
                                    "p (f hh) -> p f hh", hh=16)
                                lhsb = lh[:, 128:152, hx:hx + 1].squeeze(2)
                                rhs = vt2[par][:, tsub * H * 65 + hx * 65:
                                               tsub * H * 65 + (hx + 1) * 65]
                                e.matmul(ps7[0:24, 195 + j * 65:195 + (j + 1) * 65],
                                         lhsb, rhs,
                                         start=(tsub == 0), stop=(tsub == NSUB - 1))
                        return None
                    dep_upd = [F["tk_phk"], F["tk_v"]]
                    if ST["upd_prev"] is not None:
                        dep_upd.append(ST["upd_prev"])
                    tk_updmm = P.add("t", dep_upd, t_upd)
                    if h == H - 1:
                        ST["war_phk"] = tk_updmm

                    def t_S_add(e, hh_rd=hh_rd):
                        h0, nw = hh_rd[0], len(hh_rd) * 65
                        P.chain(e, "v", e.tensor_tensor(
                            Sa[:, h0 * 65:h0 * 65 + nw],
                            Sa[:, h0 * 65:h0 * 65 + nw],
                            ps7[:, 0:nw], ALU.add))
                        return e.tensor_tensor(
                            Sb[:, h0 * 65:h0 * 65 + nw],
                            Sb[:, h0 * 65:h0 * 65 + nw],
                            ps7[0:24, 195:195 + nw], ALU.add)
                    tk_sadd = P.add("v", [tk_updmm, tk_wv1], t_S_add)
                    ST["upd_prev"] = tk_sadd
                    tk_heads_done.append((tk_sadd, tk_sadd))

                if True:
                    tk_pte = None
                    if sc > 0 and hp == 0:
                        pb = (h // 2) % 2
                        bank_pt = pstT if pb == 0 else pstT2

                        def t_phiT(e, h=h, bank_pt=bank_pt):
                            for hx in (h, h + 1):
                                hb = (hx - h) * 512
                                for q2 in range(NSUB):
                                    sl = phq[:, q2 * PHI * 16:(q2 + 1) * PHI * 16].rearrange(
                                        "p (f hh) -> p f hh", hh=16)[:, :, hx:hx + 1].squeeze(2)
                                    e.transpose(bank_pt[:, hb + q2 * 256:hb + q2 * 256 + 128],
                                                sl[:, 0:128], id16[:, :])
                                    e.transpose(bank_pt[0:24, hb + q2 * 256 + 128:
                                                        hb + q2 * 256 + 256],
                                                sl[:, 128:152], id16[:, :])
                            return None
                        dep_pt = [F[f"tk_phq{NSUB - 1}"]]
                        if ST["last_pte"][pb] is not None:
                            dep_pt.append(ST["last_pte"][pb])
                        tk_pt = P.add("t", dep_pt, t_phiT)
                        ST["war_phq"] = tk_pt

                        def t_phiT_ev(e, bank_pt=bank_pt):
                            return e.tensor_copy(phqT[:, 0:1024], bank_pt[:, 0:1024])
                        dep_pte = [tk_pt]
                        if ST["last_omm_pt"][0] is not None:
                            dep_pte.append(ST["last_omm_pt"][0])
                        tk_pte = P.add("v", dep_pte, t_phiT_ev)
                        ST["last_pte"][pb] = tk_pte
                        ST["pte_cur"] = tk_pte

                    def t_o_mm2(e, h=h, hp=hp, sc=sc,
                                snap_prev=snap_prev, par=par):
                        reg = ps[4] if hp == 0 else ps[1]
                        for qb in range(NSUB):
                            oc = qb * 65
                            n_mm = (qb + 1) + (2 if sc > 0 else 0)
                            i = 0
                            for kb in range(qb + 1):
                                lhs = ssq[:, hp * SSW + sso[kb] + (qb - kb) * 128:
                                          hp * SSW + sso[kb] + (qb - kb) * 128 + 128]
                                rhs = vt2[par][:, kb * H * 65 + h * 65:
                                               kb * H * 65 + (h + 1) * 65]
                                e.matmul(reg[:, oc:oc + 65], lhs, rhs,
                                         start=(i == 0), stop=(i == n_mm - 1))
                                i += 1
                            if sc > 0:
                                hb = (h % 2) * 512
                                e.matmul(reg[:, oc:oc + 65],
                                         phqT[:, hb + qb * 256:hb + qb * 256 + 128],
                                         snap_prev[0][:, h * 65:(h + 1) * 65],
                                         start=False, stop=False)
                                e.matmul(reg[:, oc:oc + 65],
                                         phqT[0:24, hb + qb * 256 + 128:hb + qb * 256 + 256],
                                         snap_prev[1][:, h * 65:(h + 1) * 65],
                                         start=False, stop=True)
                        return None
                    dep_omm = [tk_sq]
                    if tk_pte is not None:
                        dep_omm.append(tk_pte)
                    if sc > 0:
                        dep_omm.append(ST["snap_written"][1 - par])
                    if len(ST["o_evac"]) >= 2:
                        dep_omm.append(ST["o_evac"][-2])
                    tk_omm = P.add("t", dep_omm, t_o_mm2)
                    ST["snap_readers"][1 - par] = tk_omm
                    ST["omm_ssq_war"][hp] = tk_omm
                    ST["last_omm_pt"][0] = tk_omm
                    if h == H - 1:
                        ST["war_vt"][par] = tk_omm
                        ST["war_qhp"][par] = last_sT

                    def mk_oz(h=h, hp=hp, tk_omm=tk_omm, par=par):
                        def t_oz_s(e, hp=hp):
                            reg = ps[4] if hp == 0 else ps[1]
                            zs = reg[:, 0:NSUB * 65].rearrange(
                                "p (qb c) -> p qb c", qb=NSUB)[:, :, 64:65].squeeze(2)
                            P.chain(e, "s", e.activation(
                                zsc[:, hp * NSUB:(hp + 1) * NSUB], zs, AF.Ln,
                                bias=cstc[:, C_LNEPS:C_LNEPS + 1]))
                            return e.activation(zsc[:, hp * NSUB:(hp + 1) * NSUB],
                                                zsc[:, hp * NSUB:(hp + 1) * NSUB],
                                                AF.Exp, scale=-1.0)
                        dep_oz = [tk_omm]
                        if ST["oz2_prev"] is not None:
                            dep_oz.append(ST["oz2_prev"])
                        tk_oz2 = P.add("s", dep_oz, t_oz_s)
                        ST["oz2_prev"] = tk_oz2

                        def t_oz3(e, h=h, hp=hp, par=par):
                            reg = ps[4] if hp == 0 else ps[1]
                            srco = reg[:, 0:NSUB * 65].rearrange(
                                "p (qb c) -> p qb c", qb=NSUB)[:, :, 0:64]
                            dst = o162[par][:, :].rearrange(
                                "p (qb c) -> p qb c", qb=NSUB)[:, :, h * 64:(h + 1) * 64]
                            rz = zsc[:, hp * NSUB:(hp + 1) * NSUB].unsqueeze(2).broadcast_to(
                                [128, NSUB, 64])
                            return e.tensor_tensor(dst, srco, rz, ALU.mult)
                        dep_oz3 = [tk_oz2]
                        if h < 2 and ST["war_o16"][par] is not None:
                            dep_oz3.append(ST["war_o16"][par])
                        tk_oz3 = P.add("v", dep_oz3, t_oz3)
                        ST["o_evac"].append(tk_oz3)
                    pending_oz[0] = mk_oz

                if weave is not None:
                    for _ in range(4):
                        next(weave, None)
            if pending_oz[0] is not None:
                pending_oz[0]()
                pending_oz[0] = None
            if sc < NSC - 1:
                def t_snap_all(e, snap_cur=snap_cur):
                    e.tensor_scalar(snap_cur[0][:, 0:H * 65], Sa[:, 0:H * 65],
                                    phcoc[:, 0:1], None, ALU.mult)
                    return e.tensor_scalar(snap_cur[1][:, 0:H * 65], Sb[:, 0:H * 65],
                                           phcoc[0:24, 1:2], None, ALU.mult)
                dep_snap = [tk_heads_done[-1][0]]
                if ST["snap_readers"][par] is not None:
                    dep_snap.append(ST["snap_readers"][par])
                tk_snapall = P.add("v", dep_snap, t_snap_all)
                ST["snap_written"][par] = tk_snapall
            F["tk_attn"] = ST["o_evac"][-1]

        def tail(sc):
            par = sc % 2
            F = FR[sc]
            tk_oT = None
            for grp in range(NSUB * 2):
                bi = grp % 2

                def t_ot(e, grp=grp, par=par):
                    bank = pstT if grp % 2 == 0 else pstT2
                    for i in range(4):
                        idx = grp * 4 + i
                        qb, hd = idx % NSUB, idx // NSUB
                        e.transpose(bank[:, i * 128:(i + 1) * 128],
                                    o162[par][:, qb * 1024 + hd * 128:qb * 1024 + (hd + 1) * 128],
                                    id16[:, :])
                    return None
                deps_ot = [F["tk_attn"]]
                if ST["last_pte"][bi] is not None:
                    deps_ot.append(ST["last_pte"][bi])
                tk_ott = P.add("t", deps_ot, t_ot)
                if grp == NSUB * 2 - 1:
                    ST["war_o16"][par] = tk_ott

                def t_ote(e, grp=grp):
                    bank = pstT if grp % 2 == 0 else pstT2
                    for i in range(4):
                        idx = grp * 4 + i
                        qb, hd = idx % NSUB, idx // NSUB
                        e.tensor_copy(oT[:, hd * SC + qb * 128:hd * SC + (qb + 1) * 128],
                                      bank[:, i * 128:(i + 1) * 128])
                    return None
                dep_ote = [tk_ott]
                if tk_oT is not None:
                    dep_ote.append(tk_oT)
                if ST["war_oT"] is not None and grp == 0:
                    dep_ote.append(ST["war_oT"])
                tk_oT = P.add("v", dep_ote, t_ote)
                ST["last_pte"][bi] = tk_oT
                yield

            X = xs2[par]
            tk_res_toks = []
            last_wo = None
            for ftile in range(NFT):
                def t_wo(e, ftile=ftile):
                    reg = psB[ftile % 2]
                    for kt in range(8):
                        e.matmul(reg[:, 0:SC],
                                 wo16[:, kt * 1024 + ftile * 128:kt * 1024 + (ftile + 1) * 128],
                                 oT[:, kt * SC:kt * SC + SC],
                                 start=(kt == 0), stop=(kt == 7))
                    return None
                dep_wo = depB(ftile % 2, tk_oT)
                if sc == 0 and ftile < 2:
                    dep_wo.append(tk_wof)
                tk_womm = P.add("t", dep_wo, t_wo)
                last_wo = tk_womm

                def t_res(e, ftile=ftile, X=X):
                    reg = psB[ftile % 2]
                    return e.tensor_tensor(X[:, ftile * SC:(ftile + 1) * SC],
                                           X[:, ftile * SC:(ftile + 1) * SC],
                                           reg[:, 0:SC], ALU.add)
                tk_res = P.add("v", [tk_womm], t_res)
                ST["war_B"][ftile % 2] = tk_res
                tk_res_toks.append(tk_res)
                yield
            ST["war_oT"] = last_wo
            tk_x2 = tk_res_toks[-1]

            def t_x2c(e, X=X, par=par):
                return e.tensor_copy(x162[par][:, :], X[:, :])
            tk_x2c = P.add("v", [tk_x2] + tk_res_toks, t_x2c)

            def t_sx2(e, par=par):
                for ft in range(NFT):
                    e.matmul(psB[0][0:1, 0:SC], o16c[:, 0:1],
                             x162[par][:, ft * SC:(ft + 1) * SC],
                             start=(ft == 0), stop=(ft == NFT - 1))
                return None
            tk_sx2 = P.add("t", depB(0, tk_x2c), t_sx2)
            ST["x16_war"][par] = tk_sx2
            yield

            def t_x2sq(e, X=X, par=par):
                return e.tensor_tensor(xsq2[par][:, :], X[:, :], X[:, :], ALU.mult)
            dep_x2sq = [tk_x2]
            if ST["war_xsq"] is not None:
                dep_x2sq.append(ST["war_xsq"])
            tk_x2sq = P.add("v", dep_x2sq, t_x2sq)

            def t_sxx2(e, par=par):
                for ft in range(NFT):
                    e.matmul(psB[1][0:1, 0:SC], o16c[:, 0:1],
                             xsq2[par][:, ft * SC:(ft + 1) * SC],
                             start=(ft == 0), stop=(ft == NFT - 1))
                return None
            tk_sxx2 = P.add("t", depB(1, tk_x2sq), t_sxx2)
            ST["war_xsq"] = tk_sxx2
            yield

            def t_l2v(e, par=par):
                sr = srow2[par]
                P.chain(e, "v", e.tensor_scalar(sr[0:1, 0:SC], psB[0][0:1, 0:SC],
                                                -1.0 / D, None, ALU.mult))
                P.chain(e, "v", e.tensor_tensor(sr[0:1, 2 * SC:3 * SC],
                                                sr[0:1, 0:SC], sr[0:1, 0:SC],
                                                ALU.mult))
                e.scalar_tensor_tensor(sr[0:1, SC:2 * SC], psB[1][0:1, 0:SC],
                                       1.0 / D, sr[0:1, 2 * SC:3 * SC],
                                       ALU.mult, ALU.subtract)
                return e.tensor_copy(nm2x[0:1, :], sr[0:1, 0:SC])
            dep_l2v = [tk_sx2, tk_sxx2]
            if ST["war_nm2"] is not None:
                dep_l2v.append(ST["war_nm2"])
            tk_l2v = P.add("v", dep_l2v, t_l2v)
            ST["war_B"][0] = tk_l2v
            ST["war_B"][1] = tk_l2v

            def t_l2s(e, par=par):
                sr = srow2[par]
                P.chain(e, "s", e.activation(sr[0:1, 2 * SC:3 * SC],
                                             sr[0:1, SC:2 * SC], AF.Ln, bias=LN_EPS))
                e.activation(sr[0:1, 2 * SC:3 * SC], sr[0:1, 2 * SC:3 * SC],
                             AF.Exp, scale=-0.5)
                return None
            tk_l2s = P.add("s", [tk_l2v], t_l2s)

            def t_l2f(e, par=par):
                return e.tensor_copy(rs2x[0:1, :], srow2[par][0:1, 2 * SC:3 * SC])
            tk_l2f = P.add("v", [tk_l2s], t_l2f)
            yield

            def t_l2bc(e):
                e.matmul(psB[0][:, 0:SC], e0[:, 0:128], nm2x[:, 0:SC],
                         start=True, stop=True)
                e.matmul(psB[1][:, 0:SC], e0[:, 0:128], rs2x[:, 0:SC],
                         start=True, stop=True)
                return None
            tk_l2bc = P.add("t", depB(0, tk_l2f) + depB(1), t_l2bc)
            ST["war_nm2"] = tk_l2bc

            tk_l2bce = tk_l2bc
            yield

            tk_q_last = None
            for fp in range(NFT // 4):
                def t_q8(e, fp=fp, X=X, par=par):
                    a0 = fp * 4 * SC
                    t2 = t1q[:, 0:4 * SC].rearrange("p (f t) -> p f t", f=4)
                    xv2 = X[:, a0:a0 + 4 * SC].rearrange("p (f t) -> p f t", f=4)
                    nmb = psB[0][:, 0:SC].unsqueeze(1).broadcast_to([128, 4, SC])
                    rsb = psB[1][:, 0:SC].unsqueeze(1).broadcast_to([128, 4, SC])
                    P.chain(e, "v", e.tensor_tensor(t2, xv2, nmb, ALU.add))
                    P.chain(e, "v", e.scalar_tensor_tensor(t2, t2, sinv[:, 0:1],
                                                           rsb, ALU.mult, ALU.mult))
                    P.chain(e, "v", e.tensor_scalar(q8f[:, 0:4 * SC], t1q[:, 0:4 * SC],
                                                    240.0, -240.0, ALU.min, ALU.max))
                    return e.tensor_copy(xq16[:, a0:a0 + 4 * SC], q8f[:, 0:4 * SC])
                dep_q8 = [tk_l2bce]
                if fp == 0 and ST.get("xq_war") is not None:
                    dep_q8.append(ST["xq_war"])
                tk_q_last = P.add("v", dep_q8, t_q8)
                ST["war_B"][0] = tk_q_last
                ST["war_B"][1] = tk_q_last
                yield
                yield
            store_toks = []
            for ftile in range(NFT):
                def t_ffn(e, ftile=ftile, par=par):
                    reg = psB[ftile % 2]
                    for kt in range(8):
                        e.matmul(reg[:, 0:SC],
                                 wf16[:, kt * 1024 + ftile * 128:kt * 1024 + (ftile + 1) * 128],
                                 xq16[:, kt * SC:kt * SC + SC],
                                 start=(kt == 0), stop=(kt == 7))
                    return None
                dep_ffn = depB(ftile % 2, tk_q_last)
                if sc == 0 and ftile < 2:
                    dep_ffn.append(tk_wof)
                tk_fmm = P.add("t", dep_ffn, t_ffn)
                if ftile == NFT - 1:
                    ST["xq_war"] = tk_fmm

                def t_yep(e, ftile=ftile, X=X):
                    reg = psB[ftile % 2]
                    yb = ybuf[:, (ftile % 2) * SC:(ftile % 2 + 1) * SC]
                    P.chain(e, "v", e.scalar_tensor_tensor(
                        yb, reg[:, 0:SC], ssc[:, 0:1],
                        X[:, ftile * SC:(ftile + 1) * SC], ALU.mult, ALU.add))
                    return e.tensor_scalar(yb, yb, biasc[:, ftile:ftile + 1],
                                           None, ALU.add)
                dep_yep = [tk_fmm]
                if ST["ybuf_war"][ftile % 2] is not None:
                    dep_yep.append(ST["ybuf_war"][ftile % 2])
                tk_yep = P.add("v", dep_yep, t_yep)
                ST["war_B"][ftile % 2] = tk_yep

                semn = "st" if ftile % 2 == 0 else "st1"

                def t_store(e, ftile=ftile, sc=sc, semn=semn):
                    e.dma_start(
                        out=yv[:, ftile, sc * SC:(sc + 1) * SC],
                        in_=ybuf[:, (ftile % 2) * SC:(ftile % 2 + 1) * SC],
                    ).then_inc(P.sems[semn], 16)
                tk_st = P.dma("q", [tk_yep], t_store, semn, 1)
                store_toks.append(tk_st)
                ST["ybuf_war"][ftile % 2] = tk_st
                yield
            ST["last_x_consumer"][par] = store_toks[-1]

        def chain_gens(gs):
            for g in gs:
                for _ in g:
                    yield

        g0 = front14(0)
        for _ in g0:
            pass
        front5(0)
        tail_prev = None
        for sc in range(NSC):
            gens = []
            if tail_prev is not None:
                gens.append(tail_prev)
            if sc + 1 < NSC:
                gens.append(front14(sc + 1))
            weave = chain_gens(gens) if gens else None
            attention(sc, weave=weave)
            if weave is not None:
                for _ in weave:
                    pass
            tail_prev = tail(sc)
            for _ in range(4):
                next(tail_prev, None)
            if sc + 1 < NSC:
                front5(sc + 1)
        for _ in tail_prev:
            pass

        def t_final(e):
            e.wait_ge(P.sems["st"], 16 * P.dma_counts["st"])
            e.wait_ge(P.sems["st1"], 16 * P.dma_counts["st1"])
        P.tasks["q"].append(([], lambda e: (t_final(e), None)[1], ("st", 0)))

        block = ent(nc.Block())
        P.emit(block)
    return nc


def _prep_core_inputs(b, x, Wq, Wk, Wv, Wo, gamma, beta, w8, bias, s_in, s_w):
    d = {}
    d["xT"] = np.ascontiguousarray(x[b].T.astype(np.float32))
    d["wq"] = np.ascontiguousarray(Wq.T)
    d["wk"] = np.ascontiguousarray(Wk.T)
    d["wv"] = np.ascontiguousarray(Wv.T)
    d["wo"] = np.ascontiguousarray(Wo.T)
    d["wf"] = np.ascontiguousarray(w8.T)
    d["g_col"] = np.ascontiguousarray(np.tile(gamma, 8)[:, None].astype(np.float32))
    d["b_col"] = np.ascontiguousarray(np.tile(beta, 8)[:, None].astype(np.float32))
    d["bias_c"] = np.ascontiguousarray(bias.reshape(8, 128).T.astype(np.float32))
    d["s_io"] = np.array([[s_in.reshape(-1)[0], s_w.reshape(-1)[0]]], np.float32)
    _co = [2.0] * 136 + [-1.0] * 16
    _pl = np.zeros((128, 128), np.float16)
    for _hl in range(8):
        for _fd in range(16):
            _pl[_hl * 16 + _fd, _fd * 8 + _hl] = 1
    d["pil"] = _pl
    _pc = np.zeros((128, 2), np.float32)
    _pc[:, 0] = _co[0:128]
    _pc[0:24, 1] = _co[128:152]
    d["phco"] = _pc
    _tr = np.triu(np.ones((128, 128), np.float16))
    d["tri"] = np.concatenate(
        [_tr, np.ones((128, 128), np.float16), _tr], axis=1)
    d["id16"] = np.eye(128, dtype=np.float16)
    ind8 = np.zeros((8, 128), np.float16)
    for g in range(8):
        ind8[g, g * 16:(g + 1) * 16] = 1
    d["ind8"] = ind8
    onesg = np.zeros((128, 8), np.float16)
    for g in range(8):
        onesg[g * 16:(g + 1) * 16, g] = 1
    d["onesg"] = onesg
    e0 = np.zeros((128, 128), np.float16)
    e0[0, :] = 1
    d["e0"] = e0
    d["o16c"] = np.ones((128, 1), np.float16)
    d["csth"] = np.broadcast_to(
        np.array([LN_EPS, 0.25, 1.0, 0.0, -0.5, -1.0, 240.0, 16.0 * LN_EPS,
                  16.0 * LN_EPS * LN_EPS], np.float32), (128, 9)).copy()
    return d


def kernel(L0_attn_x, L0_attn_Wq, L0_attn_Wk, L0_attn_Wv, L0_attn_Wo,
           L0_attn_gamma, L0_attn_beta, L0_ffn_weight_fp8, L0_ffn_bias,
           L0_ffn_input_scale, L0_ffn_weight_scale):
    args = (np.asarray(L0_attn_x, np.float32), np.asarray(L0_attn_Wq, np.float32),
            np.asarray(L0_attn_Wk, np.float32), np.asarray(L0_attn_Wv, np.float32),
            np.asarray(L0_attn_Wo, np.float32), np.asarray(L0_attn_gamma, np.float32),
            np.asarray(L0_attn_beta, np.float32), np.asarray(L0_ffn_weight_fp8, np.float32),
            np.asarray(L0_ffn_bias, np.float32), np.asarray(L0_ffn_input_scale, np.float32),
            np.asarray(L0_ffn_weight_scale, np.float32))
    x = args[0]
    B, L, _ = x.shape
    nc = build(L=L)
    in_maps = [_prep_core_inputs(b, *args) for b in range(B)]
    res = run_bass_kernel_spmd(nc, in_maps, core_ids=list(range(B)))
    out = np.empty((B, L, D), np.float32)
    for b in range(B):
        out[b] = res.results[b]["yT"].T
    return out
